# version 07
# baseline (speedup 1.0000x reference)
import numpy as np
import concourse.bass as bass
import concourse.mybir as mybir
from concourse.bass_utils import run_bass_kernel_spmd

F32 = mybir.dt.float32
BF16 = mybir.dt.bfloat16
I32 = mybir.dt.int32
OP = mybir.AluOpType
ACTF = mybir.ActivationFunctionType
AX = mybir.AxisListType.X

T = 16384
D = 1024
HID = 4096
NCORE = 8
OWN = T // NCORE
HALO = 64
KT = 64
R = OWN + 2
M = OWN + HALO + 1
NB = 8
BL = OWN // NB
CL = 128
NCC = OWN // CL
HC = 256
NHC = HID // HC
CCH = 8
HCH = HID // 128
NT_X1 = 17

BLEN = [BL] * (NB - 1) + [BL + 2]
FLEN = [BL + HALO - 1] + [BL] * (NB - 2) + [BL + 2]
FOFF = [0] + [HALO - 1] * (NB - 1)
FSRC = [0] + [BL * j + HALO - 1 for j in range(1, NB)]
MBW = HALO - 1 + BL + 2

ENGINES = ("sync", "tensor", "vector", "scalar", "gpsimd")
DMA_SEMS = ("lc", "lx", "lw1", "lw2", "liw", "low", "lq1", "lq2", "lpr",
            "sw", "so")


DRAIN_SEMS = ("ve", "se", "ge")


class Prog:
    def __init__(self):
        self.ops = {e: [] for e in ENGINES}
        self.cnt = {s: 0 for s in DMA_SEMS}
        self.cnt.update({"pe": 0, "ve": 0, "se": 0, "ge": 0})
        self.final = False

    def emit(self, eng, fn, waits=(), inc=None):
        if inc is None:
            tok = None
        else:
            self.cnt[inc] += 1
            tok = (inc, self.cnt[inc])
        self.ops[eng].append((fn, [t for t in waits if t is not None], tok))
        return tok

    def finalize(self):
        used = set()
        for e in ENGINES:
            for _, waits, _ in self.ops[e]:
                used.update(waits)
        kept = {s: [] for s in DRAIN_SEMS}
        for e in ENGINES:
            for _, _, tok in self.ops[e]:
                if tok is not None and tok[0] in DRAIN_SEMS and tok in used:
                    kept[tok[0]].append(tok[1])
        for s in kept:
            kept[s].sort()
        import bisect
        def remap(tok):
            s, v = tok
            if s not in DRAIN_SEMS:
                return tok
            return (s, bisect.bisect_right(kept[s], v))
        newops = {}
        for e in ENGINES:
            lst = []
            for fn, waits, tok in self.ops[e]:
                nw = [remap(t) for t in waits]
                if tok is None:
                    lst.append((fn, nw, None, False))
                elif tok[0] in DRAIN_SEMS:
                    if tok in used:
                        lst.append((fn, nw, remap(tok), True))
                    else:
                        lst.append((fn, nw, None, False))
                else:
                    lst.append((fn, nw, tok, False))
            newops[e] = lst
        self.ops = newops
        for s in DRAIN_SEMS:
            self.cnt[s] = len(kept[s])
        self.final = True

    def run(self, name, eng, sems):
        assert self.final
        seen = {}
        for fn, waits, tok, drained in self.ops[name]:
            for (s, v) in waits:
                if seen.get(s, 0) < v:
                    eng.wait_ge(sems[s], 16 * v if s in DMA_SEMS else v)
                    seen[s] = v
            inst = fn(eng)
            if tok is not None:
                if drained:
                    eng.drain().then_inc(sems[tok[0]], 1)
                else:
                    inst.then_inc(sems[tok[0]], 16 if tok[0] in DMA_SEMS else 1)


def build_program(debug_outputs=False):
    nc = bass.Bass("TRN2", target_bir_lowering=False,
                   detect_race_conditions=False)

    def din(name, shape, dt=F32):
        return nc.dram_tensor(name, shape, dt, kind="ExternalInput")

    xF_h = din("xF", [D, M])
    inw_h = din("in_w", [D, D])
    mw1_h = din("mod_w1", [D, HID])
    mw2_h = din("mod_w2", [HID, D])
    outw_h = din("out_w", [D, D])
    hyk_h = din("hyk", [D, KT])
    w1t_h = din("w1t", [D, HID])
    w2t_h = din("w2t", [HID, D])
    cw3_h = din("cw3", [HID, 3])
    cb_h = din("cb", [HID])
    b1_h = din("b1", [HID])
    b2_h = din("b2", [D])
    inb_h = din("inb", [D])
    outb_h = din("outb", [D])
    idf_h = din("idf", [128, 128])
    ones_h = din("ones_row", [1, 128])
    onesc_h = din("ones_col", [128, 1])
    maskL_h = din("maskL", [128, 1])
    maskR_h = din("maskR", [128, 1])
    out_h = nc.dram_tensor("out", [OWN, D], F32, kind="ExternalOutput")

    s_w1h = nc.dram_tensor("s_w1h", [D, HID], BF16, kind="Internal")
    s_w1l = nc.dram_tensor("s_w1l", [D, HID], BF16, kind="Internal")
    s_w2h = nc.dram_tensor("s_w2h", [HID, D], BF16, kind="Internal")
    s_w2l = nc.dram_tensor("s_w2l", [HID, D], BF16, kind="Internal")
    s_inwh = nc.dram_tensor("s_inwh", [D, D], BF16, kind="Internal")
    s_inwl = nc.dram_tensor("s_inwl", [D, D], BF16, kind="Internal")
    s_owh = nc.dram_tensor("s_owh", [D, D], BF16, kind="Internal")
    s_owl = nc.dram_tensor("s_owl", [D, D], BF16, kind="Internal")
    s_q1 = nc.dram_tensor("s_q1", [D, HID], BF16, kind="Internal")
    s_q2 = nc.dram_tensor("s_q2", [HID, D], BF16, kind="Internal")

    A = nc.alloc_sbuf_tensor
    hyk = A("hyk_s", [128, CCH, KT], F32)
    cw3 = A("cw3_s", [128, HCH, 3], F32)
    cbs = A("cb_s", [128, HCH], F32)
    b1s = A("b1_s", [128, HCH], F32)
    b2s = A("b2_s", [128, CCH], F32)
    inbs = A("inb_s", [128, CCH], F32)
    outbs = A("outb_s", [128, CCH], F32)
    idf = A("idf_s", [128, 128], F32)
    ones_r = A("ones_r", [1, 128], F32)
    ones_c = A("ones_c", [128, 1], F32)
    maskL = A("maskL_s", [128, 1], F32)
    maskR = A("maskR_s", [128, 1], F32)
    sw1b = A("sw1b", [128, 1], F32)
    sw2b = A("sw2b", [128, 1], F32)
    mw1b = A("mw1b", [128, 1], F32)
    mw2b = A("mw2b", [128, 1], F32)
    d1t = A("d1t", [128, NT_X1 + 1], F32)
    acc1 = A("acc1", [128, 1], F32)
    acc2 = A("acc2", [128, 1], F32)
    acc_tmp = A("acc_tmp", [128, 1], F32)
    tinyA = A("tinyA", [128, 8], F32)
    tinyB = A("tinyB", [128, 8], F32)
    tinyD = A("tinyD", [128, 8], F32)
    tinyE = A("tinyE", [128, 8], F32)
    tinyC = A("tinyC", [1, 8], F32)

    x1r = [A(f"x1r{s}", [128, CCH, BL + 2], F32) for s in range(2)]
    xqr = [A(f"xqr{s}", [128, CCH, BL + 2], BF16) for s in range(2)]

    HXW = (MBW + 1) // 2
    xbf = A("xbf", [128, CCH, HXW], F32)
    xh = [A(f"xh{s}", [128, CCH, MBW], BF16) for s in range(2)]
    xl = [A(f"xl{s}", [128, CCH, MBW], BF16) for s in range(2)]
    mbuf = [A(f"mbuf{s}", [128, CCH, MBW], F32) for s in range(2)]
    w1ch = A("w1ch", [128, CCH, HC], BF16)
    w1cl = A("w1cl", [128, CCH, HC], BF16)
    w2ch = A("w2ch", [128, 2, D], BF16)
    w2cl = A("w2cl", [128, 2, D], BF16)
    g32 = A("g32", [128, 2, MBW], F32)
    gh = [A(f"gh{s}", [128, 2, MBW], BF16) for s in range(2)]
    gl = [A(f"gl{s}", [128, 2, MBW], BF16) for s in range(2)]
    inwch = A("inwch", [128, CCH, 128], BF16)
    inwcl = A("inwcl", [128, CCH, 128], BF16)

    y32 = A("y32", [128, CCH, BL + 2], F32)
    yh = A("yh", [128, CCH, BL + 2], BF16)
    yl = A("yl", [128, CCH, BL + 2], BF16)
    owh = A("owh", [128, CCH, 128], BF16)
    owl = A("owl", [128, CCH, 128], BF16)
    zfm = A("zfm", [128, CCH, BL + 2], F32)
    ztm = A("ztm", [128, D], F32)
    x1t = A("x1t", [128, D], F32)
    uq = A("uq", [128, D], F32)
    ciq = A("ciq", [128, D], I32)
    gpq = A("gpq", [128, D], BF16)

    htile = A("htile", [128, HCH, CL + 2], F32)
    qht = A("qht", [128, HCH, CL], BF16)
    w1q = A("w1q", [128, CCH, 128], BF16)
    w2q = A("w2q", [128, HCH, 128], BF16)
    z2fm = A("z2fm", [128, CCH, CL], F32)
    shb = A("shb", [128, CL], F32)
    dhb = A("dhb", [128, CL], F32)
    d1b = A("d1b", [128, CL + 2], F32)
    M1 = A("M1", [128, CL], F32)
    uc = A("uc", [128, CL], F32)
    cic = A("cic", [128, CL], I32)
    gpc = A("gpc", [128, CL], BF16)
    mxt = A("mxt", [128, 1], F32)
    sht = A("sht", [128, 1], F32)
    dht = A("dht", [128, 1], F32)
    d1row = A("d1row", [1, CL + 2], F32)
    shrow = A("shrow", [1, CL], F32)
    dhrow = A("dhrow", [1, CL], F32)

    htf = htile[:].rearrange("p a b -> p (a b)")
    qhf = qht[:].rearrange("p a b -> p (a b)")
    wfp = htf[:, 0:1024].rearrange("p (a b) -> p a b", a=CCH)
    upv = htf[:, 1024:2048].rearrange("p (a b) -> p a b", a=CCH)
    gppv = htf[:, 2048:3072].rearrange("p (a b) -> p a b", a=CCH)
    gnpv = htf[:, 3072:4096].rearrange("p (a b) -> p a b", a=CCH)
    qwp = qhf[:, 0:2048].rearrange("p (a b) -> p a b", a=CCH)
    x2tv = htf[:, 0:1024]
    ztm2 = htf[:, 1024:2048]

    ps = [nc.alloc_psum_tensor(f"ps{i}", [128, 512], F32) for i in range(8)]

    prog = Prog()
    E = prog.emit

    def mm_group(bank, mms, waits=(), inc="pe"):
        def fn(te, bank=bank, mms=list(mms)):
            last = None
            n = len(mms)
            for i, (lh, rh, cs) in enumerate(mms):
                last = te.matmul(ps[bank][:, cs], lh, rh,
                                 start=(i == 0), stop=(i == n - 1))
            return last
        return E("tensor", fn, waits, inc)

    def transpose(bank_ap, src_ap, waits=(), inc="pe"):
        k = src_ap.shape[0]
        def fn(te, b=bank_ap, s=src_ap, k=k):
            return te.matmul(b, s, idf[0:k, 0:k], is_transpose=True,
                             start=True, stop=True)
        return E("tensor", fn, waits, inc)

    def dma(eng, out_ap, in_ap, waits=(), slow=False, sem=None):
        if sem is None:
            sem = "lc" if eng == "sync" else "so"
        def fn(e, o=out_ap, i=in_ap, slow=slow):
            return e.dma_start(out=o, in_=i, allow_slow_non_contiguous=slow)
        return E(eng, fn, waits, sem)

    def vop(fn, waits=(), inc="ve"):
        return E("vector", fn, waits, inc)

    def sop(fn, waits=(), inc="se"):
        return E("scalar", fn, waits, inc)

    def passes3(wh, wl, ah, al):
        return [(wh, ah), (wh, al), (wl, ah)]

    st = {"gfree": [None, None], "g_ready": [None, None]}

    dma("sync", idf[:], idf_h.ap())
    t_ones = dma("sync", ones_r[:], ones_h.ap())
    t_onesc = dma("sync", ones_c[:], onesc_h.ap())
    t_mskL = dma("sync", maskL[:], maskL_h.ap())
    t_mskR = dma("sync", maskR[:], maskR_h.ap())
    t_hyk = dma("sync", hyk[:], hyk_h.ap().rearrange("(c p) k -> p c k", p=128))
    t_cw3 = dma("sync", cw3[:], cw3_h.ap().rearrange("(c p) k -> p c k", p=128))
    t_cb = dma("sync", cbs[:], cb_h.ap().rearrange("(c p) -> p c", p=128),
               slow=True)
    t_b1 = dma("sync", b1s[:], b1_h.ap().rearrange("(c p) -> p c", p=128),
               slow=True)
    t_b2 = dma("sync", b2s[:], b2_h.ap().rearrange("(c p) -> p c", p=128),
               slow=True)
    t_inb = dma("sync", inbs[:], inb_h.ap().rearrange("(c p) -> p c", p=128),
                slow=True)
    t_outb = dma("sync", outbs[:], outb_h.ap().rearrange("(c p) -> p c", p=128),
                 slow=True)

    def wchunks(wh_, cols):
        nrows, ncols = wh_.shape
        for rc in range(nrows // 1024):
            base = wh_.ap()[rc * 1024:(rc + 1) * 1024, :] \
                .rearrange("(c p) n -> p c n", p=128)
            for cc in range(ncols // cols):
                yield rc, cc, base[:, :, cc * cols:(cc + 1) * cols]

    def schunk(scr, rc, cc, cols):
        return scr.ap()[rc * 1024:(rc + 1) * 1024, :] \
            .rearrange("(c p) n -> p c n", p=128)[:, :, cc * cols:(cc + 1) * cols]

    t_prev = None
    scr_w1_tok = [None] * NHC
    scr_w2_tok = [None] * NHC
    PW = 128

    def hilo_pass(src_h, dsth, dstl, tok_arr=None, per=0):
        nonlocal t_prev
        for rc, cc, chunk in wchunks(src_h, PW):
            tl = dma("sync", wfp[:, :, 0:PW], chunk, sem="lpr",
                     waits=[t_prev] if t_prev else [])
            th = sop(lambda s: s.activation(qwp[:, :, 0:PW], wfp[:, :, 0:PW],
                                            ACTF.Copy), [tl])
            tlo = vop(lambda v: v.tensor_tensor(qwp[:, :, PW:2 * PW],
                                                wfp[:, :, 0:PW],
                                                qwp[:, :, 0:PW],
                                                op=OP.subtract), [tl, th])
            dma("scalar", schunk(dsth, rc, cc, PW), qwp[:, :, 0:PW],
                waits=[th], sem="sw")
            t_prev = dma("scalar", schunk(dstl, rc, cc, PW),
                         qwp[:, :, PW:2 * PW], waits=[tlo], sem="sw")
            if tok_arr is not None:
                idx = (rc * (src_h.shape[1] // PW) + cc) // per
                tok_arr[idx] = t_prev

    hilo_pass(mw1_h, s_w1h, s_w1l, scr_w1_tok, 2)
    hilo_pass(mw2_h, s_w2h, s_w2l, scr_w2_tok, 2)
    hilo_pass(inw_h, s_inwh, s_inwl)
    hilo_pass(outw_h, s_owh, s_owl)
    tok_hilo = t_prev

    vop(lambda v: v.memset(acc1[:], 0.0))
    vop(lambda v: v.memset(acc2[:], 0.0))
    tok_abs = []
    for wi, (wh_, accs) in enumerate([(w1t_h, acc1), (w2t_h, acc2)]):
        for rc, cc, chunk in wchunks(wh_, PW):
            tl = dma("sync", wfp[:, :, 0:PW], chunk, sem="lpr", waits=[t_prev])
            t_ab = sop(lambda s: s.activation(upv[:, :, 0:PW],
                                              wfp[:, :, 0:PW], ACTF.Abs),
                       [tl])
            t_prev = vop(lambda v, a=accs: (
                v.reduce_sum(acc_tmp[:],
                             upv[:, :, 0:PW].rearrange("p a b -> p (a b)"),
                             axis=AX),
                v.drain(),
                v.tensor_tensor(a[:], a[:], acc_tmp[:], op=OP.add))[-1],
                [t_ab])
        tok_abs.append(t_prev)

    def sum_fn(te):
        te.matmul(ps[6][0:1, 0:1], acc1[:], ones_c[:], start=True, stop=True)
        return te.matmul(ps[6][0:1, 1:2], acc2[:], ones_c[:], start=True,
                         stop=True)
    t_sum = E("tensor", sum_fn, [tok_abs[1], t_onesc], "pe")

    def mkscales(v):
        v.tensor_scalar(tinyC[0:1, 0:1], ps[6][0:1, 0:1], 1.0 / (D * HID),
                        1e-5, op0=OP.mult, op1=OP.max)
        v.tensor_scalar(tinyC[0:1, 1:2], ps[6][0:1, 1:2], 1.0 / (D * HID),
                        1e-5, op0=OP.mult, op1=OP.max)
        v.drain()
        return v.reciprocal(tinyC[0:1, 2:4], tinyC[0:1, 0:2])
    t_sc = vop(mkscales, [t_sum])
    t_bc = mm_group(6, [(ones_r[:], tinyC[0:1, 0:4], slice(0, 4))],
                    waits=[t_sc, t_ones])
    t_mw = vop(lambda v: (v.tensor_copy(mw1b[:], ps[6][:, 0:1]),
                          v.tensor_copy(mw2b[:], ps[6][:, 1:2]),
                          v.tensor_copy(sw1b[:], ps[6][:, 2:3]),
                          v.tensor_copy(sw2b[:], ps[6][:, 3:4]))[-1], [t_bc])
    bank_free = [None] * 8
    bank_free[6] = t_mw

    t_prev = t_mw
    for wi, (wh_, sc, scr) in enumerate([(w1t_h, sw1b, s_q1),
                                         (w2t_h, sw2b, s_q2)]):
        for rc, cc, chunk in wchunks(wh_, PW):
            tl = dma("sync", wfp[:, :, 0:PW], chunk, sem="lpr", waits=[t_prev])
            tu = vop(lambda v, sc=sc: v.tensor_scalar(
                upv[:, :, 0:PW], wfp[:, :, 0:PW], sc[:, 0:1], None,
                op0=OP.mult), [tl, t_mw])
            tg = vop(lambda v: v.tensor_single_scalar(
                gppv[:, :, 0:PW], upv[:, :, 0:PW], 0.5, op=OP.is_ge), [tu])
            tn = vop(lambda v: v.tensor_single_scalar(
                gnpv[:, :, 0:PW], upv[:, :, 0:PW], -0.5, op=OP.is_le), [tu])
            tq = vop(lambda v: v.tensor_tensor(
                qwp[:, :, 0:PW], gppv[:, :, 0:PW], gnpv[:, :, 0:PW],
                op=OP.subtract), [tg, tn])
            t_prev = dma("scalar", schunk(scr, rc, cc, PW), qwp[:, :, 0:PW],
                         waits=[tq], sem="sw")
    tok_scratch = t_prev

    tok_m_ready = [None] * NB
    tok_conv_done = [None] * NB
    tok_x_ready = [None] * NB
    tok_xfree = [None] * NB
    tok_x1_ready = [None] * NB
    tok_x1_free = [None] * NB
    tok_y32_free = [None] * NB
    tok_yhl_free = [None] * NB
    tok_xq_first = [None] * NB
    tok_d1_ready = [None] * NB
    tok_store = []

    def emit_A(j):
        slot = j % 2
        fs, fl, fo = FSRC[j], FLEN[j], FOFF[j]
        xv = xF_h.ap().rearrange("(c p) m -> p c m", p=128)
        t_lo = None
        off = 0
        while off < fl:
            ln = min(HXW, fl - off)
            t_xl = dma("sync", xbf[:, :, 0:ln],
                       xv[:, :, fs + off:fs + off + ln], sem="lx",
                       waits=[st.get("xbf_free"), t_hyk])
            t_hi = sop(lambda s, off=off, ln=ln: s.activation(
                xh[slot][:, :, fo + off:fo + off + ln], xbf[:, :, 0:ln],
                ACTF.Copy),
                [t_xl, tok_xfree[j - 2] if j >= 2 else None])
            t_lo = vop(lambda v, off=off, ln=ln: v.tensor_tensor(
                xl[slot][:, :, fo + off:fo + off + ln], xbf[:, :, 0:ln],
                xh[slot][:, :, fo + off:fo + off + ln], op=OP.subtract),
                [t_xl, t_hi, tok_xfree[j - 2] if j >= 2 else None])
            st["xbf_free"] = t_lo
            off += ln
        tok_x_ready[j] = t_lo

        t_seam = None
        if j > 0:
            t_seam = vop(lambda v: v.tensor_copy(
                mbuf[slot][:, :, 0:HALO - 1],
                mbuf[(j - 1) % 2][:, :, BL:BL + HALO - 1]),
                [tok_m_ready[j - 1],
                 tok_conv_done[j - 2] if j >= 2 else None])

        m_acc = [None] * CCH

        def emit_mod(hc, rot, g_ready, w2_tok):
            for mo in range(CCH):
                bank = 2 + (mo % 2)
                mms = []
                for (wop, gop) in passes3(w2ch, w2cl, gh[rot], gl[rot]):
                    for kc in range(2):
                        mms.append((wop[:, kc, mo * 128:(mo + 1) * 128],
                                    gop[:, kc, 0:fl], slice(0, fl)))
                tmod = mm_group(bank, mms,
                                waits=[w2_tok, g_ready, bank_free[bank]])
                if hc == 0:
                    tev = sop(lambda s, mo=mo, bank=bank: s.activation(
                        mbuf[slot][:, mo, fo:fo + fl], ps[bank][:, 0:fl],
                        ACTF.Copy),
                        [tmod, t_seam,
                         tok_conv_done[j - 2] if j >= 2 else None])
                elif hc == NHC - 1:
                    tev = vop(lambda v, mo=mo, bank=bank:
                              v.scalar_tensor_tensor(
                                  mbuf[slot][:, mo, fo:fo + fl],
                                  ps[bank][:, 0:fl], b2s[:, mo:mo + 1],
                                  mbuf[slot][:, mo, fo:fo + fl],
                                  op0=OP.add, op1=OP.add),
                              [tmod, t_b2, m_acc[mo]])
                else:
                    tev = vop(lambda v, mo=mo, bank=bank: v.tensor_tensor(
                        mbuf[slot][:, mo, fo:fo + fl], ps[bank][:, 0:fl],
                        mbuf[slot][:, mo, fo:fo + fl], op=OP.add),
                        [tmod, m_acc[mo]])
                m_acc[mo] = tev
                bank_free[bank] = tev
            st["gfree"][rot] = bank_free[2 + ((CCH - 1) % 2)]
            st["w2c_free"] = bank_free[2 + ((CCH - 1) % 2)]

        for hc in range(NHC + 1):
            if hc < NHC:
                dma("sync", w1ch[:],
                    s_w1h.ap().rearrange("(c p) h -> p c h", p=128)
                    [:, :, hc * HC:(hc + 1) * HC], sem="lw1",
                    waits=[st.get("w1c_free"), scr_w1_tok[hc]])
                w1_tok = dma("sync", w1cl[:],
                             s_w1l.ap().rearrange("(c p) h -> p c h", p=128)
                             [:, :, hc * HC:(hc + 1) * HC], sem="lw1")
                rot = hc % 2
                g_evs = []
                for mc in range(2):
                    bank = mc
                    mms = []
                    for (wop, xop) in passes3(w1ch, w1cl, xh[slot], xl[slot]):
                        for kd in range(CCH):
                            mms.append((wop[:, kd, mc * 128:(mc + 1) * 128],
                                        xop[:, kd, fo:fo + fl], slice(0, fl)))
                    tg = mm_group(bank, mms,
                                  waits=[w1_tok, tok_x_ready[j],
                                         bank_free[bank]])
                    te = sop(lambda s, mc=mc, hc=hc: s.activation(
                        g32[:, mc, 0:fl], ps[mc][:, 0:fl], ACTF.Gelu,
                        bias=b1s[:, 2 * hc + mc:2 * hc + mc + 1], scale=1.0),
                        [tg, t_b1, st.get("g32_free")])
                    bank_free[bank] = te
                    g_evs.append(te)
                st["w1c_free"] = g_evs[1]
                tgh = sop(lambda s, rot=rot: s.activation(
                    gh[rot][:, :, 0:fl], g32[:, :, 0:fl], ACTF.Copy),
                    [g_evs[0], g_evs[1], st["gfree"][rot]])
                tgl = vop(lambda v, rot=rot: v.tensor_tensor(
                    gl[rot][:, :, 0:fl], g32[:, :, 0:fl],
                    gh[rot][:, :, 0:fl], op=OP.subtract),
                    [tgh, st["gfree"][rot]])
                st["g32_free"] = tgl
            if hc >= 1:
                phc, prot, pg, pw2 = st["pend_mod"]
                emit_mod(phc, prot, pg, pw2)
            if hc < NHC:
                dma("sync", w2ch[:],
                    s_w2h.ap()[hc * HC:(hc + 1) * HC, :]
                    .rearrange("(c p) d -> p c d", p=128), sem="lw2",
                    waits=[st.get("w2c_free"), scr_w2_tok[hc]])
                w2_tok = dma("sync", w2cl[:],
                             s_w2l.ap()[hc * HC:(hc + 1) * HC, :]
                             .rearrange("(c p) d -> p c d", p=128), sem="lw2")
                st["pend_mod"] = (hc, rot, tgl, w2_tok)

        t_m = None
        for mo in range(CCH):
            bank = 2 + (mo % 2)
            dma("sync", inwch[:],
                s_inwh.ap().rearrange("(c p) d -> p c d", p=128)
                [:, :, mo * 128:(mo + 1) * 128], sem="liw",
                waits=[tok_hilo, st.get("inw_free")])
            tiw = dma("sync", inwcl[:],
                      s_inwl.ap().rearrange("(c p) d -> p c d", p=128)
                      [:, :, mo * 128:(mo + 1) * 128], sem="liw")
            mms = []
            for (wop, xop) in passes3(inwch, inwcl, xh[slot], xl[slot]):
                for kd in range(CCH):
                    mms.append((wop[:, kd, :],
                                xop[:, kd, fo:fo + fl], slice(0, fl)))
            txp = mm_group(bank, mms,
                           waits=[tiw, tok_x_ready[j], bank_free[bank]])
            st["inw_free"] = txp
            t_m = vop(lambda v, mo=mo, bank=bank: v.scalar_tensor_tensor(
                mbuf[slot][:, mo, fo:fo + fl], ps[bank][:, 0:fl],
                inbs[:, mo:mo + 1], mbuf[slot][:, mo, fo:fo + fl],
                op0=OP.add, op1=OP.mult), [txp, t_inb, m_acc[mo]])
            m_acc[mo] = t_m
            bank_free[bank] = t_m
        tok_m_ready[j] = m_acc[CCH - 1]

    def emit_B(j):
        slot = j % 2
        bl = BLEN[j]
        t_conv = None
        for c in range(CCH):
            t_conv = vop(lambda v, c=c: v.tensor_scalar(
                y32[:, c, 0:bl],
                mbuf[slot][:, c, HALO - 1:HALO - 1 + bl],
                hyk[:, c, 0:1], None, op0=OP.mult),
                [tok_m_ready[j],
                 tok_y32_free[j - 1] if j >= 1 else None])
        for k in range(1, KT):
            for c in range(CCH):
                t_conv = vop(lambda v, c=c, k=k: v.scalar_tensor_tensor(
                    y32[:, c, 0:bl],
                    mbuf[slot][:, c, HALO - 1 - k:HALO - 1 - k + bl],
                    hyk[:, c, k:k + 1], y32[:, c, 0:bl],
                    op0=OP.mult, op1=OP.add))
        tok_conv_done[j] = t_conv
        t_yh = sop(lambda s: s.activation(yh[:, :, 0:bl],
                                          y32[:, :, 0:bl], ACTF.Copy),
                   [t_conv, tok_yhl_free[j - 1] if j >= 1 else None])
        t_yl = vop(lambda v: v.tensor_tensor(yl[:, :, 0:bl],
                                             y32[:, :, 0:bl],
                                             yh[:, :, 0:bl],
                                             op=OP.subtract),
                   [t_yh, tok_yhl_free[j - 1] if j >= 1 else None])
        tok_y32_free[j] = t_yl
        tok_y_ready = t_yl

        t_z = None
        last_tz = None
        for mo in range(CCH):
            dma("sync", owh[:],
                s_owh.ap().rearrange("(c p) d -> p c d", p=128)
                [:, :, mo * 128:(mo + 1) * 128], sem="low",
                waits=[tok_hilo, st.get("ow_free")])
            towl = dma("sync", owl[:],
                       s_owl.ap().rearrange("(c p) d -> p c d", p=128)
                       [:, :, mo * 128:(mo + 1) * 128], sem="low")
            mms = []
            for (wop, yop) in passes3(owh, owl, yh, yl):
                for kc in range(CCH):
                    mms.append((wop[:, kc, :], yop[:, kc, 0:bl],
                                slice(0, bl)))
            tz = mm_group(4, mms, waits=[towl, tok_y_ready, bank_free[4]])
            st["ow_free"] = tz
            last_tz = tz
            t1 = vop(lambda v, mo=mo: v.scalar_tensor_tensor(
                zfm[:, mo, 0:bl], ps[4][:, 0:bl], outbs[:, mo:mo + 1],
                xh[slot][:, mo, HALO - 1:HALO - 1 + bl],
                op0=OP.add, op1=OP.add),
                [tz, t_outb, tok_x_ready[j], st.get("zfm_free")])
            t_z = vop(lambda v, mo=mo: v.tensor_tensor(
                zfm[:, mo, 0:bl], zfm[:, mo, 0:bl],
                xl[slot][:, mo, HALO - 1:HALO - 1 + bl], op=OP.add), [t1])
            bank_free[4] = t_z
        tok_yhl_free[j] = last_tz
        tok_xfree[j] = t_z

        ntile = 2 if j < NB - 1 else 3
        t_ring = None
        last_zread = None
        for tt in range(ntile):
            tc0 = 128 * tt
            tw = 128 if tt < 2 else 2
            gidx = 2 * j + tt
            gate = st.get("tileB_free")
            t_ztm = None
            for o in range(CCH):
                ttr = transpose(ps[5][0:tw, 0:128], zfm[:, o, tc0:tc0 + tw],
                                waits=[t_z, bank_free[5]])
                last_zread = ttr
                t_ztm = sop(lambda s, o=o, tw=tw: s.activation(
                    ztm[0:tw, o * 128:(o + 1) * 128], ps[5][0:tw, 0:128],
                    ACTF.Copy), [ttr, gate])
                bank_free[5] = t_ztm
            t_s1 = vop(lambda v, tw=tw: v.reduce_sum(
                tinyA[0:tw, 0:1], ztm[0:tw, :], axis=AX), [t_ztm, gate])
            t_ss = sop(lambda s, tw=tw: s.activation(
                x1t[0:tw, :], ztm[0:tw, :], ACTF.Square,
                accum_out=tinyA[0:tw, 1:2]), [t_ztm, gate])

            def ln_small(v, tw=tw):
                v.tensor_scalar(tinyA[0:tw, 2:3], tinyA[0:tw, 0:1], 1.0 / D,
                                None, op0=OP.mult)
                v.tensor_scalar(tinyA[0:tw, 3:4], tinyA[0:tw, 1:2], 1.0 / D,
                                None, op0=OP.mult)
                v.drain()
                v.tensor_tensor(tinyA[0:tw, 4:5], tinyA[0:tw, 2:3],
                                tinyA[0:tw, 2:3], op=OP.mult)
                v.drain()
                v.tensor_tensor(tinyA[0:tw, 5:6], tinyA[0:tw, 3:4],
                                tinyA[0:tw, 4:5], op=OP.subtract)
                v.drain()
                return v.tensor_single_scalar(tinyA[0:tw, 5:6],
                                              tinyA[0:tw, 5:6], 1e-5,
                                              op=OP.add)
            t_var = vop(ln_small, [t_s1, t_ss])
            t_sd = sop(lambda s, tw=tw: s.activation(
                tinyB[0:tw, 0:1], tinyA[0:tw, 5:6], ACTF.Sqrt), [t_var])

            def ln_fin(v, tw=tw):
                v.reciprocal(tinyB[0:tw, 1:2], tinyB[0:tw, 0:1])
                v.drain()
                return v.scalar_tensor_tensor(
                    tinyB[0:tw, 2:3], tinyA[0:tw, 2:3], -1.0,
                    tinyB[0:tw, 1:2], op0=OP.mult, op1=OP.mult)
            t_ab = vop(ln_fin, [t_sd])
            t_x1 = vop(lambda v, tw=tw: v.tensor_scalar(
                x1t[0:tw, :], ztm[0:tw, :], tinyB[0:tw, 1:2],
                tinyB[0:tw, 2:3], op0=OP.mult, op1=OP.add), [t_ab, t_ss])
            t_mx = vop(lambda v, tw=tw: v.reduce_max(
                tinyB[0:tw, 3:4], x1t[0:tw, :], axis=AX,
                apply_absolute_value=True), [t_x1])

            def qscale(v, tw=tw, gidx=gidx):
                v.tensor_single_scalar(tinyB[0:tw, 4:5], tinyB[0:tw, 3:4],
                                       1e-5, op=OP.max)
                v.drain()
                v.reciprocal(tinyB[0:tw, 5:6], tinyB[0:tw, 4:5])
                return v.scalar_tensor_tensor(
                    d1t[0:tw, gidx:gidx + 1], tinyB[0:tw, 4:5], 1.0 / 127.0,
                    mw1b[0:tw, 0:1], op0=OP.mult, op1=OP.mult)
            t_qs = vop(qscale, [t_mx, t_mw])
            tok_d1_ready[j] = t_qs
            t_u = vop(lambda v, tw=tw: v.tensor_scalar(
                uq[0:tw, :], x1t[0:tw, :], tinyB[0:tw, 5:6], 127.0,
                op0=OP.mult, op1=OP.mult), [t_qs])
            t_ci = vop(lambda v, tw=tw: v.tensor_copy(
                ciq[0:tw, :], uq[0:tw, :]), [t_u])
            t_t2 = vop(lambda v, tw=tw: v.tensor_tensor(
                uq[0:tw, :], uq[0:tw, :], ciq[0:tw, :], op=OP.subtract),
                [t_ci])
            t_gp = vop(lambda v, tw=tw: v.tensor_single_scalar(
                gpq[0:tw, :], uq[0:tw, :], 0.5, op=OP.is_ge), [t_t2])
            t_gn = vop(lambda v, tw=tw: v.tensor_single_scalar(
                uq[0:tw, :], uq[0:tw, :], -0.5, op=OP.is_le), [t_gp])
            t_co = vop(lambda v, tw=tw: v.tensor_tensor(
                gpq[0:tw, :], gpq[0:tw, :], uq[0:tw, :], op=OP.subtract),
                [t_gp, t_gn])
            t_q = vop(lambda v, tw=tw: v.tensor_tensor(
                uq[0:tw, :], gpq[0:tw, :], ciq[0:tw, :], op=OP.add), [t_co])
            ring_w = tok_x1_free[j - 2] if j >= 2 else None
            for o in range(CCH):
                ttr = transpose(ps[5][0:128, 0:tw],
                                uq[0:tw, o * 128:(o + 1) * 128],
                                waits=[t_q, bank_free[5]])
                t_e1 = sop(lambda s, o=o, tc0=tc0, tw=tw: s.activation(
                    xqr[slot][:, o, tc0:tc0 + tw], ps[5][:, 0:tw], ACTF.Copy),
                    [ttr, ring_w])
                bank_free[5] = t_e1
                ttr2 = transpose(ps[5][0:128, 0:tw],
                                 x1t[0:tw, o * 128:(o + 1) * 128],
                                 waits=[t_x1, t_q, bank_free[5]])
                t_e2 = sop(lambda s, o=o, tc0=tc0, tw=tw: s.activation(
                    x1r[slot][:, o, tc0:tc0 + tw], ps[5][:, 0:tw], ACTF.Copy),
                    [ttr2, ring_w])
                bank_free[5] = t_e2
                t_ring = t_e2
            st["tileB_free"] = t_ring
            if tt == 0:
                tok_xq_first[j] = t_ring
        tok_x1_ready[j] = t_ring
        st["zfm_free"] = last_zread

    def emit_C(jc):
        j = jc // 2
        slot = j % 2
        half = jc % 2
        base = CL * half
        last = (jc == NCC - 1)
        split = (half == 1 and not last)
        seam_tok = tok_xq_first[j + 1] if split else tok_x1_ready[j]
        t_r = None
        for (srcap, n0, n1, wk) in (
                (d1t[:, jc:jc + 1], 0, CL, tok_d1_ready[j]),
                (d1t[0:2, jc + 1:jc + 2], CL, CL + 2,
                 tok_d1_ready[min(j + (1 if jc % 2 == 1 else 0), NB - 1)])):
            npart = n1 - n0
            ttr = transpose(ps[6][0:1, 0:npart], srcap,
                            waits=[wk, bank_free[6]])
            t_r = sop(lambda s, n0=n0, n1=n1, npart=npart: s.activation(
                d1row[0:1, n0:n1], ps[6][0:1, 0:npart], ACTF.Copy),
                [ttr, st.get("d1row_free"), st.get("tileC_free")])
            bank_free[6] = t_r
        t_bcd = mm_group(6, [(ones_r[:], d1row[0:1, 0:CL + 2],
                              slice(0, CL + 2))],
                         waits=[t_r, bank_free[6]])
        st["d1row_free"] = t_bcd
        t_d1 = vop(lambda v: v.tensor_copy(d1b[:, 0:CL + 2],
                                           ps[6][:, 0:CL + 2]), [t_bcd])
        bank_free[6] = t_d1

        t_m1z = vop(lambda v: v.memset(M1[:], 0.0), [st.get("m1_free")])
        t_max = t_m1z
        t_h_done = None
        for hx in range(HCH):
            twq = dma("sync", w1q[:],
                      s_q1.ap().rearrange("(c p) h -> p c h", p=128)
                      [:, :, hx * 128:(hx + 1) * 128], sem="lq1",
                      waits=[tok_scratch, st.get("w1q_free")])
            bank = 6 + (hx % 2)
            mms = []
            if split:
                for kd in range(CCH):
                    mms.append((w1q[:, kd, :],
                                xqr[slot][:, kd, base:base + CL],
                                slice(0, CL)))
                for kd in range(CCH):
                    mms.append((w1q[:, kd, :],
                                xqr[(j + 1) % 2][:, kd, 0:2],
                                slice(CL, CL + 2)))
            else:
                for kd in range(CCH):
                    mms.append((w1q[:, kd, :],
                                xqr[slot][:, kd, base:base + CL + 2],
                                slice(0, CL + 2)))
            th1 = mm_group(bank, mms,
                           waits=[twq, tok_x1_ready[j], seam_tok,
                                  bank_free[bank]])
            st["w1q_free"] = th1
            t_ev = vop(lambda v, hx=hx, bank=bank: v.tensor_tensor(
                htile[:, hx, 0:CL + 2], ps[bank][:, 0:CL + 2],
                d1b[:, 0:CL + 2], op=OP.mult),
                [th1, t_d1, tok_scratch if (jc == 0 and hx == 0) else None,
                 st.get("tileC_free")])
            bank_free[bank] = t_ev
            if jc == 0:
                t_ev = vop(lambda v, hx=hx: v.tensor_scalar(
                    htile[:, hx, 0:1], htile[:, hx, 0:1],
                    maskL[:, 0:1], None, op0=OP.mult), [t_ev, t_mskL])
            if last:
                t_ev = vop(lambda v, hx=hx: v.tensor_scalar(
                    htile[:, hx, CL + 1:CL + 2], htile[:, hx, CL + 1:CL + 2],
                    maskR[:, 0:1], None, op0=OP.mult), [t_ev, t_mskR])
            t_c3 = vop(lambda v, hx=hx: v.tensor_scalar(
                uc[:, 0:CL], htile[:, hx, 0:CL], cw3[:, hx, 0:1],
                None, op0=OP.mult), [t_ev, t_cw3])
            t_c3 = vop(lambda v, hx=hx: v.scalar_tensor_tensor(
                uc[:, 0:CL], htile[:, hx, 1:CL + 1], cw3[:, hx, 1:2],
                uc[:, 0:CL], op0=OP.mult, op1=OP.add), [t_c3])
            t_c3 = vop(lambda v, hx=hx: v.scalar_tensor_tensor(
                uc[:, 0:CL], htile[:, hx, 2:CL + 2], cw3[:, hx, 2:3],
                uc[:, 0:CL], op0=OP.mult, op1=OP.add), [t_c3])
            t_si = sop(lambda s, hx=hx: s.activation(
                htile[:, hx, 1:CL + 1], uc[:, 0:CL], ACTF.Silu,
                bias=cbs[:, hx:hx + 1], scale=1.0), [t_c3, t_cb])
            t_h_done = t_si
            t_ab2 = sop(lambda s, hx=hx: s.activation(
                uc[:, 0:CL], htile[:, hx, 1:CL + 1], ACTF.Abs),
                [t_si, st.get("habs_free")])
            t_max = vop(lambda v: v.tensor_tensor(
                M1[:, 0:CL], M1[:, 0:CL], uc[:, 0:CL], op=OP.max),
                [t_ab2, t_max])
            st["habs_free"] = t_max
        ttr = transpose(ps[7][0:128, 0:128], M1[:, 0:CL],
                        waits=[t_max, bank_free[7]])
        t_red = vop(lambda v: v.reduce_max(mxt[:, 0:1], ps[7][:, 0:128],
                                           axis=AX),
                    [ttr, st.get("mx_free")])
        bank_free[7] = t_red
        st["m1_free"] = t_red

        def hscale(v):
            v.tensor_single_scalar(mxt[:, 0:1], mxt[:, 0:1], 1e-5, op=OP.max)
            v.drain()
            v.reciprocal(sht[:, 0:1], mxt[:, 0:1])
            return v.scalar_tensor_tensor(dht[:, 0:1], mxt[:, 0:1],
                                          1.0 / 127.0, mw2b[:, 0:1],
                                          op0=OP.mult, op1=OP.mult)
        t_hs = vop(hscale, [t_red, t_mw])
        st["mx_free"] = t_hs
        t_rr = None
        for (srcap, dst) in ((sht[:, 0:1], shrow), (dht[:, 0:1], dhrow)):
            ttr = transpose(ps[7][0:1, 0:128], srcap,
                            waits=[t_hs, bank_free[7]])
            t_rr = sop(lambda s, dst=dst: s.activation(
                dst[0:1, 0:CL], ps[7][0:1, 0:128], ACTF.Copy),
                [ttr, st.get("shrow_free"), st.get("tileC_free")])
            bank_free[7] = t_rr
        t_bs = mm_group(7, [(ones_r[:], shrow[0:1, 0:CL], slice(0, CL))],
                        waits=[t_rr, bank_free[7]])
        t_shb = vop(lambda v: v.tensor_copy(shb[:, 0:CL], ps[7][:, 0:CL]),
                    [t_bs])
        bank_free[7] = t_shb
        t_bd = mm_group(7, [(ones_r[:], dhrow[0:1, 0:CL], slice(0, CL))],
                        waits=[t_rr, bank_free[7]])
        t_dhb = vop(lambda v: v.tensor_copy(dhb[:, 0:CL], ps[7][:, 0:CL]),
                    [t_bd])
        bank_free[7] = t_dhb
        st["shrow_free"] = t_dhb

        t_qh = None
        t_qprev = None
        for c in range(HCH):
            tu = vop(lambda v, c=c: v.tensor_tensor(
                uc[:, 0:CL], htile[:, c, 1:CL + 1], shb[:, 0:CL],
                op=OP.mult),
                [t_shb, t_h_done] if c == 0 else [t_qprev])
            tu = vop(lambda v: v.tensor_single_scalar(
                uc[:, 0:CL], uc[:, 0:CL], 127.0, op=OP.mult), [tu])
            tci = vop(lambda v: v.tensor_copy(cic[:, 0:CL], uc[:, 0:CL]),
                      [tu])
            tt2_ = vop(lambda v: v.tensor_tensor(
                uc[:, 0:CL], uc[:, 0:CL], cic[:, 0:CL], op=OP.subtract),
                [tci])
            tgp = vop(lambda v: v.tensor_single_scalar(
                gpc[:, 0:CL], uc[:, 0:CL], 0.5, op=OP.is_ge), [tt2_])
            tgn = vop(lambda v: v.tensor_single_scalar(
                uc[:, 0:CL], uc[:, 0:CL], -0.5, op=OP.is_le), [tgp])
            tco = vop(lambda v: v.tensor_tensor(
                gpc[:, 0:CL], gpc[:, 0:CL], uc[:, 0:CL], op=OP.subtract),
                [tgp, tgn])
            t_qh = vop(lambda v, c=c: v.tensor_tensor(
                qht[:, c, 0:CL], gpc[:, 0:CL], cic[:, 0:CL], op=OP.add),
                [tco])
            t_qprev = t_qh
        st["habs_free"] = t_qh

        t_z2 = None
        for mo in range(CCH):
            tw2 = dma("sync", w2q[:],
                      s_q2.ap().rearrange("(c p) d -> p c d", p=128)
                      [:, :, mo * 128:(mo + 1) * 128], sem="lq2",
                      waits=[tok_scratch, st.get("w2q_free")])
            bank = 6 + (mo % 2)
            mms = [(w2q[:, kc, :], qht[:, kc, 0:CL], slice(0, CL))
                   for kc in range(HCH)]
            th2 = mm_group(bank, mms,
                           waits=[tw2, t_qh, bank_free[bank]])
            st["w2q_free"] = th2
            t1 = vop(lambda v, mo=mo, bank=bank: v.tensor_tensor(
                z2fm[:, mo, 0:CL], ps[bank][:, 0:CL], dhb[:, 0:CL],
                op=OP.mult), [th2, t_dhb, st.get("z2_free")])
            if half == 0:
                t_z2 = vop(lambda v, mo=mo: v.tensor_tensor(
                    z2fm[:, mo, 0:CL], z2fm[:, mo, 0:CL],
                    x1r[slot][:, mo, 1:CL + 1], op=OP.add),
                    [t1, tok_x1_ready[j]])
            elif last:
                t_z2 = vop(lambda v, mo=mo: v.tensor_tensor(
                    z2fm[:, mo, 0:CL], z2fm[:, mo, 0:CL],
                    x1r[slot][:, mo, CL + 1:2 * CL + 1], op=OP.add),
                    [t1, tok_x1_ready[j]])
            else:
                t2a = vop(lambda v, mo=mo: v.tensor_tensor(
                    z2fm[:, mo, 0:CL - 1], z2fm[:, mo, 0:CL - 1],
                    x1r[slot][:, mo, CL + 1:2 * CL], op=OP.add),
                    [t1, tok_x1_ready[j]])
                t_z2 = vop(lambda v, mo=mo: v.tensor_tensor(
                    z2fm[:, mo, CL - 1:CL], z2fm[:, mo, CL - 1:CL],
                    x1r[(j + 1) % 2][:, mo, 0:1], op=OP.add),
                    [t2a, tok_xq_first[j + 1]])
            bank_free[bank] = t_z2
        if half == 1:
            tok_x1_free[j] = t_z2
        st["dhb_free"] = t_z2

        gate = st.get("tileC_free")
        t_ztm = None
        last_z2read = None
        for o in range(CCH):
            ttr = transpose(ps[7][0:128, 0:128], z2fm[:, o, 0:CL],
                            waits=[t_z2, bank_free[7]])
            last_z2read = ttr
            t_ztm = sop(lambda s, o=o: s.activation(
                ztm2[:, o * 128:(o + 1) * 128], ps[7][:, 0:128],
                ACTF.Copy), [ttr, gate])
            bank_free[7] = t_ztm
        t_s1 = vop(lambda v: v.reduce_sum(tinyD[:, 0:1], ztm2, axis=AX),
                   [t_ztm, gate])
        t_ss = sop(lambda s: s.activation(x2tv, ztm2, ACTF.Square,
                                          accum_out=tinyD[:, 1:2]),
                   [t_ztm, gate])

        def ln2_small(v):
            v.tensor_scalar(tinyD[:, 2:3], tinyD[:, 0:1], 1.0 / D, None,
                            op0=OP.mult)
            v.tensor_scalar(tinyD[:, 3:4], tinyD[:, 1:2], 1.0 / D, None,
                            op0=OP.mult)
            v.drain()
            v.tensor_tensor(tinyD[:, 4:5], tinyD[:, 2:3], tinyD[:, 2:3],
                            op=OP.mult)
            v.drain()
            v.tensor_tensor(tinyD[:, 5:6], tinyD[:, 3:4], tinyD[:, 4:5],
                            op=OP.subtract)
            v.drain()
            return v.tensor_single_scalar(tinyD[:, 5:6], tinyD[:, 5:6],
                                          1e-5, op=OP.add)
        t_var = vop(ln2_small, [t_s1, t_ss])
        t_sd = sop(lambda s: s.activation(tinyE[:, 0:1], tinyD[:, 5:6],
                                          ACTF.Sqrt), [t_var])

        def ln2_fin(v):
            v.reciprocal(tinyE[:, 1:2], tinyE[:, 0:1])
            v.drain()
            return v.scalar_tensor_tensor(
                tinyE[:, 2:3], tinyD[:, 2:3], -1.0, tinyE[:, 1:2],
                op0=OP.mult, op1=OP.mult)
        t_ab3 = vop(ln2_fin, [t_sd])
        t_out = vop(lambda v: v.tensor_scalar(
            ztm2, ztm2, tinyE[:, 1:2], tinyE[:, 2:3],
            op0=OP.mult, op1=OP.add), [t_ab3, t_ss])
        ts_ = dma("scalar", out_h.ap()[CL * jc:CL * (jc + 1), :],
                  ztm2, waits=[t_out])
        tok_store.append(ts_)
        st["tileC_free"] = ts_
        st["z2_free"] = last_z2read

    for jj in range(NB + 2):
        if jj < NB:
            emit_A(jj)
        if 0 <= jj - 1 < NB:
            emit_B(jj - 1)
        if 0 <= jj - 2 < NB:
            emit_C(2 * (jj - 2))
            emit_C(2 * (jj - 2) + 1)

    if debug_outputs:
        dbg_specs = [
            ("dbg_d1t", d1t[:], [128, NT_X1 + 1], F32),
            ("dbg_m1", mbuf[1][:], [128, CCH, MBW], F32),
            ("dbg_x1r1", x1r[1][:], [128, CCH, BL + 2], F32),
            ("dbg_xqr1", xqr[1][:], [128, CCH, BL + 2], BF16),
            ("dbg_scal", tinyC[:], [1, 8], F32),
            ("dbg_acc1", acc1[:], [128, 1], F32),
            ("dbg_acc2", acc2[:], [128, 1], F32),
            ("dbg_mw", mw1b[:], [128, 1], F32),
            ("dbg_sw", sw1b[:], [128, 1], F32),
            ("dbg_qht", qht[:], [128, HCH, CL], BF16),
            ("dbg_ht", htile[:], [128, HCH, CL + 2], F32),
        ]
        fin = [(s, c) for s, c in prog.cnt.items()
               if c > 0 and s != "ge" and s != "so"]
        for nm, ap, shp, dt in dbg_specs:
            hh = nc.dram_tensor(nm, shp, dt, kind="ExternalOutput")
            dma("scalar", hh.ap(), ap, waits=fin)
        for nm, scr, shp in [("dbg_s_w1h", s_w1h, [D, HID]),
                             ("dbg_s_q1", s_q1, [D, HID]),
                             ("dbg_s_owh", s_owh, [D, D])]:
            hh = nc.dram_tensor(nm, shp, BF16, kind="ExternalOutput")
            dma("scalar", hh.ap(), scr.ap(), waits=fin)

    prog.finalize()
    from contextlib import ExitStack
    es = ExitStack()
    sems = {k: es.enter_context(nc.semaphore(k))
            for k in list(DMA_SEMS) + ["pe", "ve", "se", "ge"]}
    blk = es.enter_context(nc.Block())

    def section(name):
        def f(eng):
            prog.run(name, eng, sems)
            if name == "scalar":
                eng.wait_ge(sems["so"], 16 * prog.cnt["so"])
                eng.wait_ge(sems["sw"], 16 * prog.cnt["sw"])
        return f

    blk.sync(section("sync"))
    blk.tensor(section("tensor"))
    blk.vector(section("vector"))
    blk.scalar(section("scalar"))
    blk.gpsimd(section("gpsimd"))
    es.close()
    return nc


_NC_CACHE = None


def kernel(**inputs):
    global _NC_CACHE
    x = np.asarray(inputs["L0_attn_x"])[0]
    xT = np.ascontiguousarray(x.T)
    w1t = np.ascontiguousarray(np.asarray(inputs["L0_ffn_w1"]).T)
    w2t = np.ascontiguousarray(np.asarray(inputs["L0_ffn_w2"]).T)
    cw3 = np.ascontiguousarray(np.asarray(inputs["L0_ffn_conv_w"])
                               .reshape(HID, 3))
    shared = {
        "in_w": np.ascontiguousarray(np.asarray(inputs["L0_attn_in_w"])),
        "mod_w1": np.ascontiguousarray(np.asarray(inputs["L0_attn_mod_w1"])),
        "mod_w2": np.ascontiguousarray(np.asarray(inputs["L0_attn_mod_w2"])),
        "out_w": np.ascontiguousarray(np.asarray(inputs["L0_attn_out_w"])),
        "hyk": np.ascontiguousarray(np.asarray(inputs["L0_attn_kernel"])),
        "w1t": w1t, "w2t": w2t, "cw3": cw3,
        "cb": np.ascontiguousarray(np.asarray(inputs["L0_ffn_conv_b"])),
        "b1": np.ascontiguousarray(np.asarray(inputs["L0_attn_mod_b1"])),
        "b2": np.ascontiguousarray(np.asarray(inputs["L0_attn_mod_b2"])),
        "inb": np.ascontiguousarray(np.asarray(inputs["L0_attn_in_b"])),
        "outb": np.ascontiguousarray(np.asarray(inputs["L0_attn_out_b"])),
        "idf": np.eye(128, dtype=np.float32),
        "ones_row": np.ones((1, 128), np.float32),
        "ones_col": np.ones((128, 1), np.float32),
    }
    in_maps = []
    for i in range(NCORE):
        a = i * OWN
        idx = np.arange(a - HALO, a + OWN + 1) % T
        xF = np.ascontiguousarray(xT[:, idx])
        if i == NCORE - 1:
            xF[:, -1] = 0.0
        m = dict(shared)
        m["xF"] = xF
        m["maskL"] = np.full((128, 1), 0.0 if i == 0 else 1.0, np.float32)
        m["maskR"] = np.full((128, 1), 0.0 if i == NCORE - 1 else 1.0,
                             np.float32)
        in_maps.append(m)

    if _NC_CACHE is None:
        _NC_CACHE = build_program()
    res = run_bass_kernel_spmd(_NC_CACHE, in_maps, core_ids=list(range(NCORE)))
    out = np.concatenate([np.asarray(r["out"]) for r in res.results], axis=0)
    return out[None, :, :]


# version 08
# speedup vs baseline: 1.1116x; 1.1116x over previous
import numpy as np
import concourse.bass as bass
import concourse.mybir as mybir
from concourse.bass_utils import run_bass_kernel_spmd

F32 = mybir.dt.float32
BF16 = mybir.dt.bfloat16
I32 = mybir.dt.int32
OP = mybir.AluOpType
ACTF = mybir.ActivationFunctionType
AX = mybir.AxisListType.X

T = 16384
D = 1024
HID = 4096
NCORE = 8
OWN = T // NCORE
HALO = 64
KT = 64
R = OWN + 2
M = OWN + HALO + 1
NB = 8
BL = OWN // NB
CL = 128
NCC = OWN // CL
HC = 256
NHC = HID // HC
CCH = 8
HCH = HID // 128
NT_X1 = 17

BLEN = [BL] * (NB - 1) + [BL + 2]
FLEN = [BL + HALO - 1] + [BL] * (NB - 2) + [BL + 2]
FOFF = [0] + [HALO - 1] * (NB - 1)
FSRC = [0] + [BL * j + HALO - 1 for j in range(1, NB)]
MBW = HALO - 1 + BL + 2

ENGINES = ("sync", "tensor", "vector", "scalar", "gpsimd")
DMA_SEMS = ("lc", "lx", "lw1", "lw2", "liw", "low", "lq1", "lq2", "lpr",
            "sw", "so")


DRAIN_SEMS = ("ve", "se", "ge")
OWN_SEM = {"vector": "ve", "scalar": "se", "tensor": "pe", "gpsimd": "ge"}


def K(tok):
    return None if tok is None else ("KEEP", tok)


class Prog:
    def __init__(self):
        self.ops = {e: [] for e in ENGINES}
        self.cnt = {s: 0 for s in DMA_SEMS}
        self.cnt.update({"pe": 0, "ve": 0, "se": 0, "ge": 0})
        self.final = False

    def emit(self, eng, fn, waits=(), inc=None):
        if inc is None:
            tok = None
        else:
            self.cnt[inc] += 1
            tok = (inc, self.cnt[inc])
        own = OWN_SEM.get(eng)
        w2 = []
        for t in waits:
            if t is None:
                continue
            if t[0] == "KEEP":
                w2.append(t[1])
            elif t[0] != own:
                w2.append(t)
        self.ops[eng].append((fn, w2, tok))
        return tok

    def finalize(self):
        used = set()
        for e in ENGINES:
            for _, waits, _ in self.ops[e]:
                used.update(waits)
        kept = {s: [] for s in DRAIN_SEMS}
        for e in ENGINES:
            for _, _, tok in self.ops[e]:
                if tok is not None and tok[0] in DRAIN_SEMS and tok in used:
                    kept[tok[0]].append(tok[1])
        for s in kept:
            kept[s].sort()
        import bisect
        def remap(tok):
            s, v = tok
            if s not in DRAIN_SEMS:
                return tok
            return (s, bisect.bisect_right(kept[s], v))
        newops = {}
        for e in ENGINES:
            lst = []
            for fn, waits, tok in self.ops[e]:
                nw = [remap(t) for t in waits]
                if tok is None:
                    lst.append((fn, nw, None, False))
                elif tok[0] in DRAIN_SEMS:
                    if tok in used:
                        lst.append((fn, nw, remap(tok), True))
                    else:
                        lst.append((fn, nw, None, False))
                else:
                    lst.append((fn, nw, tok, False))
            newops[e] = lst
        self.ops = newops
        for s in DRAIN_SEMS:
            self.cnt[s] = len(kept[s])
        self.final = True

    def run(self, name, eng, sems):
        assert self.final
        seen = {}
        for fn, waits, tok, drained in self.ops[name]:
            for (s, v) in waits:
                if seen.get(s, 0) < v:
                    eng.wait_ge(sems[s], 16 * v if s in DMA_SEMS else v)
                    seen[s] = v
            inst = fn(eng)
            if tok is not None:
                if drained:
                    eng.drain().then_inc(sems[tok[0]], 1)
                else:
                    inst.then_inc(sems[tok[0]], 16 if tok[0] in DMA_SEMS else 1)


def build_program(debug_outputs=False):
    nc = bass.Bass("TRN2", target_bir_lowering=False,
                   detect_race_conditions=False)

    def din(name, shape, dt=F32):
        return nc.dram_tensor(name, shape, dt, kind="ExternalInput")

    xF_h = din("xF", [D, M])
    inw_h = din("in_w", [D, D])
    mw1_h = din("mod_w1", [D, HID])
    mw2_h = din("mod_w2", [HID, D])
    outw_h = din("out_w", [D, D])
    hyk_h = din("hyk", [D, KT])
    w1t_h = din("w1t", [D, HID])
    w2t_h = din("w2t", [HID, D])
    cw3_h = din("cw3", [HID, 3])
    cb_h = din("cb", [HID])
    b1_h = din("b1", [HID])
    b2_h = din("b2", [D])
    inb_h = din("inb", [D])
    outb_h = din("outb", [D])
    idf_h = din("idf", [128, 128])
    ones_h = din("ones_row", [1, 128])
    onesc_h = din("ones_col", [128, 1])
    maskL_h = din("maskL", [128, 1])
    maskR_h = din("maskR", [128, 1])
    out_h = nc.dram_tensor("out", [OWN, D], F32, kind="ExternalOutput")

    s_w1h = nc.dram_tensor("s_w1h", [D, HID], BF16, kind="Internal")
    s_w1l = nc.dram_tensor("s_w1l", [D, HID], BF16, kind="Internal")
    s_w2h = nc.dram_tensor("s_w2h", [HID, D], BF16, kind="Internal")
    s_w2l = nc.dram_tensor("s_w2l", [HID, D], BF16, kind="Internal")
    s_inwh = nc.dram_tensor("s_inwh", [D, D], BF16, kind="Internal")
    s_inwl = nc.dram_tensor("s_inwl", [D, D], BF16, kind="Internal")
    s_owh = nc.dram_tensor("s_owh", [D, D], BF16, kind="Internal")
    s_owl = nc.dram_tensor("s_owl", [D, D], BF16, kind="Internal")
    s_q1 = nc.dram_tensor("s_q1", [D, HID], BF16, kind="Internal")
    s_q2 = nc.dram_tensor("s_q2", [HID, D], BF16, kind="Internal")

    A = nc.alloc_sbuf_tensor
    hyk = A("hyk_s", [128, CCH, KT], F32)
    cw3 = A("cw3_s", [128, HCH, 3], F32)
    cbs = A("cb_s", [128, HCH], F32)
    b1s = A("b1_s", [128, HCH], F32)
    b2s = A("b2_s", [128, CCH], F32)
    inbs = A("inb_s", [128, CCH], F32)
    outbs = A("outb_s", [128, CCH], F32)
    idf = A("idf_s", [128, 128], F32)
    ones_r = A("ones_r", [1, 128], F32)
    ones_c = A("ones_c", [128, 1], F32)
    maskL = A("maskL_s", [128, 1], F32)
    maskR = A("maskR_s", [128, 1], F32)
    sw1b = A("sw1b", [128, 1], F32)
    sw2b = A("sw2b", [128, 1], F32)
    mw1b = A("mw1b", [128, 1], F32)
    mw2b = A("mw2b", [128, 1], F32)
    d1t = A("d1t", [128, NT_X1 + 1], F32)
    acc1 = A("acc1", [128, 1], F32)
    acc2 = A("acc2", [128, 1], F32)
    acc_tmp = A("acc_tmp", [128, 1], F32)
    tinyA = A("tinyA", [128, 8], F32)
    tinyB = A("tinyB", [128, 8], F32)
    tinyD = A("tinyD", [128, 8], F32)
    tinyE = A("tinyE", [128, 8], F32)
    tinyC = A("tinyC", [1, 8], F32)

    x1r = [A(f"x1r{s}", [128, CCH, BL + 2], F32) for s in range(2)]
    xqr = [A(f"xqr{s}", [128, CCH, BL + 2], BF16) for s in range(2)]

    HXW = (MBW + 1) // 2
    xbf = A("xbf", [128, CCH, HXW], F32)
    xh = [A(f"xh{s}", [128, CCH, MBW], BF16) for s in range(2)]
    xl = [A(f"xl{s}", [128, CCH, MBW], BF16) for s in range(2)]
    mbuf = [A(f"mbuf{s}", [128, CCH, MBW], F32) for s in range(2)]
    w1ch = A("w1ch", [128, CCH, HC], BF16)
    w1cl = A("w1cl", [128, CCH, HC], BF16)
    w2ch = A("w2ch", [128, 2, D], BF16)
    w2cl = A("w2cl", [128, 2, D], BF16)
    g32 = A("g32", [128, 2, MBW], F32)
    gh = [A(f"gh{s}", [128, 2, MBW], BF16) for s in range(2)]
    gl = [A(f"gl{s}", [128, 2, MBW], BF16) for s in range(2)]
    inwch = A("inwch", [128, CCH, 128], BF16)
    inwcl = A("inwcl", [128, CCH, 128], BF16)

    y32 = A("y32", [128, CCH, BL + 2], F32)
    yh = A("yh", [128, CCH, BL + 2], BF16)
    yl = A("yl", [128, CCH, BL + 2], BF16)
    owh = A("owh", [128, CCH, 128], BF16)
    owl = A("owl", [128, CCH, 128], BF16)
    zfm = A("zfm", [128, CCH, BL + 2], F32)
    ztm = A("ztm", [128, D], F32)
    x1t = A("x1t", [128, D], F32)
    uq = A("uq", [128, D], F32)
    ciq = A("ciq", [128, D], I32)
    gpq = A("gpq", [128, D], BF16)

    htile_f = A("htile", [128, HCH * (CL + 2) + 8], F32)
    htile = htile_f[:, 0:HCH * (CL + 2)].rearrange("p (a b) -> p a b", a=HCH)
    qht = A("qht", [128, HCH, CL], BF16)
    w1q = A("w1q", [128, CCH, 128], BF16)
    w2q = A("w2q", [128, HCH, 128], BF16)
    z2fm = A("z2fm", [128, CCH, CL], F32)
    shb = A("shb", [128, CL], F32)
    dhb = A("dhb", [128, CL], F32)
    d1b = A("d1b", [128, 136], F32)
    M1 = A("M1", [128, CL], F32)
    uc = A("uc", [128, CL], F32)
    cic = A("cic", [128, CL], I32)
    gpc = A("gpc", [128, CL], BF16)
    mxt = A("mxt", [128, 1], F32)
    sht = A("sht", [128, 1], F32)
    dht = A("dht", [128, 1], F32)
    d1row = A("d1row", [1, CL + 2], F32)
    shrow = A("shrow", [1, CL], F32)
    dhrow = A("dhrow", [1, CL], F32)

    htf = htile_f[:, 0:HCH * (CL + 2)]
    qhf = qht[:].rearrange("p a b -> p (a b)")
    wfp = htf[:, 0:1024].rearrange("p (a b) -> p a b", a=CCH)
    upv = htf[:, 1024:2048].rearrange("p (a b) -> p a b", a=CCH)
    gppv = htf[:, 2048:3072].rearrange("p (a b) -> p a b", a=CCH)
    gnpv = htf[:, 3072:4096].rearrange("p (a b) -> p a b", a=CCH)
    qwp = qhf[:, 0:2048].rearrange("p (a b) -> p a b", a=CCH)
    x2tv = htf[:, 0:1024]
    ztm2 = htf[:, 1024:2048]

    ps = [nc.alloc_psum_tensor(f"ps{i}", [128, 512], F32) for i in range(8)]

    prog = Prog()
    E = prog.emit

    def mm_group(bank, mms, waits=(), inc="pe"):
        def fn(te, bank=bank, mms=list(mms)):
            last = None
            n = len(mms)
            for i, (lh, rh, cs) in enumerate(mms):
                last = te.matmul(ps[bank][:, cs], lh, rh,
                                 start=(i == 0), stop=(i == n - 1))
            return last
        return E("tensor", fn, waits, inc)

    def transpose(bank_ap, src_ap, waits=(), inc="pe"):
        k = src_ap.shape[0]
        def fn(te, b=bank_ap, s=src_ap, k=k):
            return te.matmul(b, s, idf[0:k, 0:k], is_transpose=True,
                             start=True, stop=True)
        return E("tensor", fn, waits, inc)

    def dma(eng, out_ap, in_ap, waits=(), slow=False, sem=None):
        if sem is None:
            sem = "lc" if eng == "sync" else "so"
        def fn(e, o=out_ap, i=in_ap, slow=slow):
            return e.dma_start(out=o, in_=i, allow_slow_non_contiguous=slow)
        return E(eng, fn, waits, sem)

    def vop(fn, waits=(), inc="ve"):
        return E("vector", fn, waits, inc)

    def sop(fn, waits=(), inc="se"):
        return E("scalar", fn, waits, inc)

    def passes3(wh, wl, ah, al):
        return [(wh, ah), (wh, al), (wl, ah)]

    st = {"gfree": [None, None], "g_ready": [None, None]}

    dma("sync", idf[:], idf_h.ap())
    t_ones = dma("sync", ones_r[:], ones_h.ap())
    t_onesc = dma("sync", ones_c[:], onesc_h.ap())
    t_mskL = dma("sync", maskL[:], maskL_h.ap())
    t_mskR = dma("sync", maskR[:], maskR_h.ap())
    t_hyk = dma("sync", hyk[:], hyk_h.ap().rearrange("(c p) k -> p c k", p=128))
    t_cw3 = dma("sync", cw3[:], cw3_h.ap().rearrange("(c p) k -> p c k", p=128))
    t_cb = dma("sync", cbs[:], cb_h.ap().rearrange("(c p) -> p c", p=128),
               slow=True)
    t_b1 = dma("sync", b1s[:], b1_h.ap().rearrange("(c p) -> p c", p=128),
               slow=True)
    t_b2 = dma("sync", b2s[:], b2_h.ap().rearrange("(c p) -> p c", p=128),
               slow=True)
    t_inb = dma("sync", inbs[:], inb_h.ap().rearrange("(c p) -> p c", p=128),
                slow=True)
    t_outb = dma("sync", outbs[:], outb_h.ap().rearrange("(c p) -> p c", p=128),
                 slow=True)

    def wchunks(wh_, cols):
        nrows, ncols = wh_.shape
        for rc in range(nrows // 1024):
            base = wh_.ap()[rc * 1024:(rc + 1) * 1024, :] \
                .rearrange("(c p) n -> p c n", p=128)
            for cc in range(ncols // cols):
                yield rc, cc, base[:, :, cc * cols:(cc + 1) * cols]

    def schunk(scr, rc, cc, cols):
        return scr.ap()[rc * 1024:(rc + 1) * 1024, :] \
            .rearrange("(c p) n -> p c n", p=128)[:, :, cc * cols:(cc + 1) * cols]

    t_prev = None
    scr_w1_tok = [None] * NHC
    scr_w2_tok = [None] * NHC
    PW = 128

    def hilo_pass(src_h, dsth, dstl, tok_arr=None, per=0):
        nonlocal t_prev
        for rc, cc, chunk in wchunks(src_h, PW):
            tl = dma("sync", wfp[:, :, 0:PW], chunk, sem="lpr",
                     waits=[t_prev] if t_prev else [])
            th = sop(lambda s: s.activation(qwp[:, :, 0:PW], wfp[:, :, 0:PW],
                                            ACTF.Copy), [tl])
            tlo = vop(lambda v: v.tensor_tensor(qwp[:, :, PW:2 * PW],
                                                wfp[:, :, 0:PW],
                                                qwp[:, :, 0:PW],
                                                op=OP.subtract), [tl, th])
            dma("scalar", schunk(dsth, rc, cc, PW), qwp[:, :, 0:PW],
                waits=[th], sem="sw")
            t_prev = dma("scalar", schunk(dstl, rc, cc, PW),
                         qwp[:, :, PW:2 * PW], waits=[tlo], sem="sw")
            if tok_arr is not None:
                idx = (rc * (src_h.shape[1] // PW) + cc) // per
                tok_arr[idx] = t_prev

    hilo_pass(mw1_h, s_w1h, s_w1l, scr_w1_tok, 2)
    hilo_pass(mw2_h, s_w2h, s_w2l, scr_w2_tok, 2)
    hilo_pass(inw_h, s_inwh, s_inwl)
    hilo_pass(outw_h, s_owh, s_owl)
    tok_hilo = t_prev

    vop(lambda v: v.memset(acc1[:], 0.0))
    vop(lambda v: v.memset(acc2[:], 0.0))
    tok_abs = []
    for wi, (wh_, accs) in enumerate([(w1t_h, acc1), (w2t_h, acc2)]):
        for rc, cc, chunk in wchunks(wh_, PW):
            tl = dma("sync", wfp[:, :, 0:PW], chunk, sem="lpr", waits=[t_prev])
            t_ab = sop(lambda s: s.activation(upv[:, :, 0:PW],
                                              wfp[:, :, 0:PW], ACTF.Abs),
                       [tl])
            t_prev = vop(lambda v, a=accs: (
                v.reduce_sum(acc_tmp[:],
                             upv[:, :, 0:PW].rearrange("p a b -> p (a b)"),
                             axis=AX),
                v.drain(),
                v.tensor_tensor(a[:], a[:], acc_tmp[:], op=OP.add))[-1],
                [t_ab])
        tok_abs.append(t_prev)

    def sum_fn(te):
        te.matmul(ps[6][0:1, 0:1], acc1[:], ones_c[:], start=True, stop=True)
        return te.matmul(ps[6][0:1, 1:2], acc2[:], ones_c[:], start=True,
                         stop=True)
    t_sum = E("tensor", sum_fn, [tok_abs[1], t_onesc], "pe")

    def mkscales(v):
        v.tensor_scalar(tinyC[0:1, 0:1], ps[6][0:1, 0:1], 1.0 / (D * HID),
                        1e-5, op0=OP.mult, op1=OP.max)
        v.tensor_scalar(tinyC[0:1, 1:2], ps[6][0:1, 1:2], 1.0 / (D * HID),
                        1e-5, op0=OP.mult, op1=OP.max)
        v.drain()
        return v.reciprocal(tinyC[0:1, 2:4], tinyC[0:1, 0:2])
    t_sc = vop(mkscales, [t_sum])
    t_bc = mm_group(6, [(ones_r[:], tinyC[0:1, 0:4], slice(0, 4))],
                    waits=[t_sc, t_ones])
    t_mw = vop(lambda v: (v.tensor_copy(mw1b[:], ps[6][:, 0:1]),
                          v.tensor_copy(mw2b[:], ps[6][:, 1:2]),
                          v.tensor_copy(sw1b[:], ps[6][:, 2:3]),
                          v.tensor_copy(sw2b[:], ps[6][:, 3:4]))[-1], [t_bc])
    bank_free = [None] * 8
    bank_free[6] = t_mw

    t_prev = t_mw
    for wi, (wh_, sc, scr) in enumerate([(w1t_h, sw1b, s_q1),
                                         (w2t_h, sw2b, s_q2)]):
        for rc, cc, chunk in wchunks(wh_, PW):
            tl = dma("sync", wfp[:, :, 0:PW], chunk, sem="lpr", waits=[t_prev])
            tu = vop(lambda v, sc=sc: v.tensor_scalar(
                upv[:, :, 0:PW], wfp[:, :, 0:PW], sc[:, 0:1], None,
                op0=OP.mult), [tl, K(t_mw)])
            tg = vop(lambda v: v.tensor_single_scalar(
                gppv[:, :, 0:PW], upv[:, :, 0:PW], 0.5, op=OP.is_ge), [tu])
            tn = vop(lambda v: v.tensor_single_scalar(
                gnpv[:, :, 0:PW], upv[:, :, 0:PW], -0.5, op=OP.is_le), [tu])
            tq = vop(lambda v: v.tensor_tensor(
                qwp[:, :, 0:PW], gppv[:, :, 0:PW], gnpv[:, :, 0:PW],
                op=OP.subtract), [tg, tn])
            t_prev = dma("scalar", schunk(scr, rc, cc, PW), qwp[:, :, 0:PW],
                         waits=[tq], sem="sw")
    tok_scratch = t_prev

    tok_m_ready = [None] * NB
    tok_conv_done = [None] * NB
    tok_x_ready = [None] * NB
    tok_xfree = [None] * NB
    tok_x1_ready = [None] * NB
    tok_x1_free = [None] * NB
    tok_y32_free = [None] * NB
    tok_yhl_free = [None] * NB
    tok_xq_first = [None] * NB
    tok_d1_ready = [None] * NB
    tok_store = []

    def emit_A(j):
        slot = j % 2
        fs, fl, fo = FSRC[j], FLEN[j], FOFF[j]
        xv = xF_h.ap().rearrange("(c p) m -> p c m", p=128)
        t_lo = None
        off = 0
        while off < fl:
            ln = min(HXW, fl - off)
            t_xl = dma("sync", xbf[:, :, 0:ln],
                       xv[:, :, fs + off:fs + off + ln], sem="lx",
                       waits=[st.get("xbf_free"), t_hyk])
            t_hi = sop(lambda s, off=off, ln=ln: s.activation(
                xh[slot][:, :, fo + off:fo + off + ln], xbf[:, :, 0:ln],
                ACTF.Copy),
                [t_xl, tok_xfree[j - 2] if j >= 2 else None])
            t_lo = vop(lambda v, off=off, ln=ln: v.tensor_tensor(
                xl[slot][:, :, fo + off:fo + off + ln], xbf[:, :, 0:ln],
                xh[slot][:, :, fo + off:fo + off + ln], op=OP.subtract),
                [t_xl, t_hi, tok_xfree[j - 2] if j >= 2 else None])
            st["xbf_free"] = t_lo
            off += ln
        tok_x_ready[j] = t_lo

        t_seam = None
        if j > 0:
            t_seam = vop(lambda v: v.tensor_copy(
                mbuf[slot][:, :, 0:HALO - 1],
                mbuf[(j - 1) % 2][:, :, BL:BL + HALO - 1]),
                [tok_m_ready[j - 1],
                 tok_conv_done[j - 2] if j >= 2 else None])

        m_acc = [None] * CCH

        def emit_mod(hc, rot, g_ready, w2_tok):
            for mo in range(CCH):
                bank = 2 + (mo % 2)
                mms = []
                for (wop, gop) in passes3(w2ch, w2cl, gh[rot], gl[rot]):
                    for kc in range(2):
                        mms.append((wop[:, kc, mo * 128:(mo + 1) * 128],
                                    gop[:, kc, 0:fl], slice(0, fl)))
                tmod = mm_group(bank, mms,
                                waits=[w2_tok, g_ready, bank_free[bank]])
                if hc == 0:
                    tev = sop(lambda s, mo=mo, bank=bank: s.activation(
                        mbuf[slot][:, mo, fo:fo + fl], ps[bank][:, 0:fl],
                        ACTF.Copy),
                        [tmod, t_seam,
                         tok_conv_done[j - 2] if j >= 2 else None])
                elif hc == NHC - 1:
                    tev = vop(lambda v, mo=mo, bank=bank:
                              v.scalar_tensor_tensor(
                                  mbuf[slot][:, mo, fo:fo + fl],
                                  ps[bank][:, 0:fl], b2s[:, mo:mo + 1],
                                  mbuf[slot][:, mo, fo:fo + fl],
                                  op0=OP.add, op1=OP.add),
                              [tmod, t_b2, m_acc[mo]])
                else:
                    tev = vop(lambda v, mo=mo, bank=bank: v.tensor_tensor(
                        mbuf[slot][:, mo, fo:fo + fl], ps[bank][:, 0:fl],
                        mbuf[slot][:, mo, fo:fo + fl], op=OP.add),
                        [tmod, m_acc[mo]])
                m_acc[mo] = tev
                bank_free[bank] = tev
            st["gfree"][rot] = bank_free[2 + ((CCH - 1) % 2)]
            st["w2c_free"] = bank_free[2 + ((CCH - 1) % 2)]

        for hc in range(NHC + 1):
            if hc < NHC:
                dma("sync", w1ch[:],
                    s_w1h.ap().rearrange("(c p) h -> p c h", p=128)
                    [:, :, hc * HC:(hc + 1) * HC], sem="lw1",
                    waits=[st.get("w1c_free"), scr_w1_tok[hc]])
                w1_tok = dma("sync", w1cl[:],
                             s_w1l.ap().rearrange("(c p) h -> p c h", p=128)
                             [:, :, hc * HC:(hc + 1) * HC], sem="lw1")
                rot = hc % 2
                g_evs = []
                for mc in range(2):
                    bank = mc
                    mms = []
                    for (wop, xop) in passes3(w1ch, w1cl, xh[slot], xl[slot]):
                        for kd in range(CCH):
                            mms.append((wop[:, kd, mc * 128:(mc + 1) * 128],
                                        xop[:, kd, fo:fo + fl], slice(0, fl)))
                    tg = mm_group(bank, mms,
                                  waits=[w1_tok, tok_x_ready[j],
                                         bank_free[bank]])
                    te = sop(lambda s, mc=mc, hc=hc: s.activation(
                        g32[:, mc, 0:fl], ps[mc][:, 0:fl], ACTF.Gelu,
                        bias=b1s[:, 2 * hc + mc:2 * hc + mc + 1], scale=1.0),
                        [tg, t_b1, st.get("g32_free")])
                    bank_free[bank] = te
                    g_evs.append(te)
                st["w1c_free"] = g_evs[1]
                tgh = sop(lambda s, rot=rot: s.activation(
                    gh[rot][:, :, 0:fl], g32[:, :, 0:fl], ACTF.Copy),
                    [g_evs[0], g_evs[1], st["gfree"][rot]])
                tgl = vop(lambda v, rot=rot: v.tensor_tensor(
                    gl[rot][:, :, 0:fl], g32[:, :, 0:fl],
                    gh[rot][:, :, 0:fl], op=OP.subtract),
                    [tgh, st["gfree"][rot]])
                st["g32_free"] = tgl
            if hc >= 1:
                phc, prot, pg, pw2 = st["pend_mod"]
                emit_mod(phc, prot, pg, pw2)
            if hc < NHC:
                dma("sync", w2ch[:],
                    s_w2h.ap()[hc * HC:(hc + 1) * HC, :]
                    .rearrange("(c p) d -> p c d", p=128), sem="lw2",
                    waits=[st.get("w2c_free"), scr_w2_tok[hc]])
                w2_tok = dma("sync", w2cl[:],
                             s_w2l.ap()[hc * HC:(hc + 1) * HC, :]
                             .rearrange("(c p) d -> p c d", p=128), sem="lw2")
                st["pend_mod"] = (hc, rot, tgl, w2_tok)

        t_m = None
        for mo in range(CCH):
            bank = 2 + (mo % 2)
            dma("sync", inwch[:],
                s_inwh.ap().rearrange("(c p) d -> p c d", p=128)
                [:, :, mo * 128:(mo + 1) * 128], sem="liw",
                waits=[tok_hilo, st.get("inw_free")])
            tiw = dma("sync", inwcl[:],
                      s_inwl.ap().rearrange("(c p) d -> p c d", p=128)
                      [:, :, mo * 128:(mo + 1) * 128], sem="liw")
            mms = []
            for (wop, xop) in passes3(inwch, inwcl, xh[slot], xl[slot]):
                for kd in range(CCH):
                    mms.append((wop[:, kd, :],
                                xop[:, kd, fo:fo + fl], slice(0, fl)))
            txp = mm_group(bank, mms,
                           waits=[tiw, tok_x_ready[j], bank_free[bank]])
            st["inw_free"] = txp
            t_m = vop(lambda v, mo=mo, bank=bank: v.scalar_tensor_tensor(
                mbuf[slot][:, mo, fo:fo + fl], ps[bank][:, 0:fl],
                inbs[:, mo:mo + 1], mbuf[slot][:, mo, fo:fo + fl],
                op0=OP.add, op1=OP.mult), [txp, t_inb, m_acc[mo]])
            m_acc[mo] = t_m
            bank_free[bank] = t_m
        tok_m_ready[j] = m_acc[CCH - 1]

    def emit_B(j):
        slot = j % 2
        bl = BLEN[j]
        t_conv = None
        for c in range(CCH):
            t_conv = vop(lambda v, c=c: v.tensor_scalar(
                y32[:, c, 0:bl],
                mbuf[slot][:, c, HALO - 1:HALO - 1 + bl],
                hyk[:, c, 0:1], None, op0=OP.mult),
                [tok_m_ready[j],
                 tok_y32_free[j - 1] if j >= 1 else None])
        for k in range(1, KT):
            for c in range(CCH):
                t_conv = vop(lambda v, c=c, k=k: v.scalar_tensor_tensor(
                    y32[:, c, 0:bl],
                    mbuf[slot][:, c, HALO - 1 - k:HALO - 1 - k + bl],
                    hyk[:, c, k:k + 1], y32[:, c, 0:bl],
                    op0=OP.mult, op1=OP.add))
        tok_conv_done[j] = t_conv
        t_yh = sop(lambda s: s.activation(yh[:, :, 0:bl],
                                          y32[:, :, 0:bl], ACTF.Copy),
                   [t_conv, tok_yhl_free[j - 1] if j >= 1 else None])
        t_yl = vop(lambda v: v.tensor_tensor(yl[:, :, 0:bl],
                                             y32[:, :, 0:bl],
                                             yh[:, :, 0:bl],
                                             op=OP.subtract),
                   [t_yh, tok_yhl_free[j - 1] if j >= 1 else None])
        tok_y32_free[j] = t_yl
        tok_y_ready = t_yl

        t_z = None
        last_tz = None
        for mo in range(CCH):
            dma("sync", owh[:],
                s_owh.ap().rearrange("(c p) d -> p c d", p=128)
                [:, :, mo * 128:(mo + 1) * 128], sem="low",
                waits=[tok_hilo, st.get("ow_free")])
            towl = dma("sync", owl[:],
                       s_owl.ap().rearrange("(c p) d -> p c d", p=128)
                       [:, :, mo * 128:(mo + 1) * 128], sem="low")
            mms = []
            for (wop, yop) in passes3(owh, owl, yh, yl):
                for kc in range(CCH):
                    mms.append((wop[:, kc, :], yop[:, kc, 0:bl],
                                slice(0, bl)))
            tz = mm_group(4, mms, waits=[towl, tok_y_ready, bank_free[4]])
            st["ow_free"] = tz
            last_tz = tz
            t1 = vop(lambda v, mo=mo: v.scalar_tensor_tensor(
                zfm[:, mo, 0:bl], ps[4][:, 0:bl], outbs[:, mo:mo + 1],
                xh[slot][:, mo, HALO - 1:HALO - 1 + bl],
                op0=OP.add, op1=OP.add),
                [tz, t_outb, tok_x_ready[j], st.get("zfm_free")])
            t_z = vop(lambda v, mo=mo: v.tensor_tensor(
                zfm[:, mo, 0:bl], zfm[:, mo, 0:bl],
                xl[slot][:, mo, HALO - 1:HALO - 1 + bl], op=OP.add), [t1])
            bank_free[4] = t_z
        tok_yhl_free[j] = last_tz
        tok_xfree[j] = t_z

        ntile = 2 if j < NB - 1 else 3
        t_ring = None
        last_zread = None
        for tt in range(ntile):
            tc0 = 128 * tt
            tw = 128 if tt < 2 else 2
            gidx = 2 * j + tt
            gate = st.get("tileB_free")
            t_ztm = None
            for o in range(CCH):
                ttr = transpose(ps[5][0:tw, 0:128], zfm[:, o, tc0:tc0 + tw],
                                waits=[t_z, bank_free[5]])
                last_zread = ttr
                t_ztm = sop(lambda s, o=o, tw=tw: s.activation(
                    ztm[0:tw, o * 128:(o + 1) * 128], ps[5][0:tw, 0:128],
                    ACTF.Copy), [ttr, gate])
                bank_free[5] = t_ztm
            t_s1 = vop(lambda v, tw=tw: v.reduce_sum(
                tinyA[0:tw, 0:1], ztm[0:tw, :], axis=AX), [t_ztm, gate])
            t_ss = sop(lambda s, tw=tw: s.activation(
                x1t[0:tw, :], ztm[0:tw, :], ACTF.Square,
                accum_out=tinyA[0:tw, 1:2]), [t_ztm, gate])

            def ln_small(v, tw=tw):
                v.tensor_scalar(tinyA[0:tw, 2:3], tinyA[0:tw, 0:1], 1.0 / D,
                                None, op0=OP.mult)
                v.tensor_scalar(tinyA[0:tw, 3:4], tinyA[0:tw, 1:2], 1.0 / D,
                                None, op0=OP.mult)
                v.drain()
                v.tensor_tensor(tinyA[0:tw, 4:5], tinyA[0:tw, 2:3],
                                tinyA[0:tw, 2:3], op=OP.mult)
                v.drain()
                v.tensor_tensor(tinyA[0:tw, 5:6], tinyA[0:tw, 3:4],
                                tinyA[0:tw, 4:5], op=OP.subtract)
                v.drain()
                return v.tensor_single_scalar(tinyA[0:tw, 5:6],
                                              tinyA[0:tw, 5:6], 1e-5,
                                              op=OP.add)
            t_var = vop(ln_small, [K(t_s1), t_ss])
            t_sd = sop(lambda s, tw=tw: s.activation(
                tinyB[0:tw, 0:1], tinyA[0:tw, 5:6], ACTF.Sqrt), [t_var])

            def ln_fin(v, tw=tw):
                v.reciprocal(tinyB[0:tw, 1:2], tinyB[0:tw, 0:1])
                v.drain()
                return v.scalar_tensor_tensor(
                    tinyB[0:tw, 2:3], tinyA[0:tw, 2:3], -1.0,
                    tinyB[0:tw, 1:2], op0=OP.mult, op1=OP.mult)
            t_ab = vop(ln_fin, [t_sd])
            t_x1 = vop(lambda v, tw=tw: v.tensor_scalar(
                x1t[0:tw, :], ztm[0:tw, :], tinyB[0:tw, 1:2],
                tinyB[0:tw, 2:3], op0=OP.mult, op1=OP.add), [K(t_ab), t_ss])
            t_mx = vop(lambda v, tw=tw: v.reduce_max(
                tinyB[0:tw, 3:4], x1t[0:tw, :], axis=AX,
                apply_absolute_value=True), [t_x1])

            def qscale(v, tw=tw, gidx=gidx):
                v.tensor_single_scalar(tinyB[0:tw, 4:5], tinyB[0:tw, 3:4],
                                       1e-5, op=OP.max)
                v.drain()
                v.reciprocal(tinyB[0:tw, 5:6], tinyB[0:tw, 4:5])
                return v.scalar_tensor_tensor(
                    d1t[0:tw, gidx:gidx + 1], tinyB[0:tw, 4:5], 1.0 / 127.0,
                    mw1b[0:tw, 0:1], op0=OP.mult, op1=OP.mult)
            t_qs = vop(qscale, [K(t_mx), K(t_mw)])
            tok_d1_ready[j] = t_qs
            t_u = vop(lambda v, tw=tw: v.tensor_scalar(
                uq[0:tw, :], x1t[0:tw, :], tinyB[0:tw, 5:6], 127.0,
                op0=OP.mult, op1=OP.mult), [K(t_qs)])
            t_ci = vop(lambda v, tw=tw: v.tensor_copy(
                ciq[0:tw, :], uq[0:tw, :]), [t_u])
            t_t2 = vop(lambda v, tw=tw: v.tensor_tensor(
                uq[0:tw, :], uq[0:tw, :], ciq[0:tw, :], op=OP.subtract),
                [t_ci])
            t_gp = vop(lambda v, tw=tw: v.tensor_single_scalar(
                gpq[0:tw, :], uq[0:tw, :], 0.5, op=OP.is_ge), [t_t2])
            t_gn = vop(lambda v, tw=tw: v.tensor_single_scalar(
                uq[0:tw, :], uq[0:tw, :], -0.5, op=OP.is_le), [t_gp])
            t_co = vop(lambda v, tw=tw: v.tensor_tensor(
                gpq[0:tw, :], gpq[0:tw, :], uq[0:tw, :], op=OP.subtract),
                [t_gp, t_gn])
            t_q = vop(lambda v, tw=tw: v.tensor_tensor(
                uq[0:tw, :], gpq[0:tw, :], ciq[0:tw, :], op=OP.add), [t_co])
            ring_w = tok_x1_free[j - 2] if j >= 2 else None
            for o in range(CCH):
                ttr = transpose(ps[5][0:128, 0:tw],
                                uq[0:tw, o * 128:(o + 1) * 128],
                                waits=[t_q, bank_free[5]])
                t_e1 = sop(lambda s, o=o, tc0=tc0, tw=tw: s.activation(
                    xqr[slot][:, o, tc0:tc0 + tw], ps[5][:, 0:tw], ACTF.Copy),
                    [ttr, ring_w])
                bank_free[5] = t_e1
                ttr2 = transpose(ps[5][0:128, 0:tw],
                                 x1t[0:tw, o * 128:(o + 1) * 128],
                                 waits=[t_x1, t_q, bank_free[5]])
                t_e2 = sop(lambda s, o=o, tc0=tc0, tw=tw: s.activation(
                    x1r[slot][:, o, tc0:tc0 + tw], ps[5][:, 0:tw], ACTF.Copy),
                    [ttr2, ring_w])
                bank_free[5] = t_e2
                t_ring = t_e2
            st["tileB_free"] = t_ring
            if tt == 0:
                tok_xq_first[j] = t_ring
        tok_x1_ready[j] = t_ring
        st["zfm_free"] = last_zread

    def emit_C(jc):
        j = jc // 2
        slot = j % 2
        half = jc % 2
        base = CL * half
        last = (jc == NCC - 1)
        split = (half == 1 and not last)
        seam_tok = tok_xq_first[j + 1] if split else tok_x1_ready[j]
        t_r = None
        for (srcap, n0, n1, wk) in (
                (d1t[:, jc:jc + 1], 0, CL, tok_d1_ready[j]),
                (d1t[0:2, jc + 1:jc + 2], CL, CL + 2,
                 tok_d1_ready[min(j + (1 if jc % 2 == 1 else 0), NB - 1)])):
            npart = n1 - n0
            ttr = transpose(ps[6][0:1, 0:npart], srcap,
                            waits=[wk, bank_free[6]])
            t_r = sop(lambda s, n0=n0, n1=n1, npart=npart: s.activation(
                d1row[0:1, n0:n1], ps[6][0:1, 0:npart], ACTF.Copy),
                [ttr, st.get("d1row_free"), st.get("tileC_free")])
            bank_free[6] = t_r
        t_bcd = mm_group(6, [(ones_r[:], d1row[0:1, 0:CL + 2],
                              slice(0, CL + 2))],
                         waits=[t_r, bank_free[6]])
        st["d1row_free"] = t_bcd
        t_d1 = vop(lambda v: v.tensor_copy(d1b[:, 0:CL + 2],
                                           ps[6][:, 0:CL + 2]), [t_bcd])
        bank_free[6] = t_d1
        if jc == 0:
            t_d1 = vop(lambda v: v.tensor_scalar(
                d1b[:, 0:1], d1b[:, 0:1], maskL[:, 0:1], None,
                op0=OP.mult), [t_mskL])
        if last:
            t_d1 = vop(lambda v: v.tensor_scalar(
                d1b[:, CL + 1:CL + 2], d1b[:, CL + 1:CL + 2],
                maskR[:, 0:1], None, op0=OP.mult), [t_mskR])

        t_m1z = vop(lambda v: v.memset(M1[:], 0.0), [st.get("m1_free")])
        t_max = t_m1z
        t_h_done = None
        for hx in range(HCH):
            twq = dma("sync", w1q[:],
                      s_q1.ap().rearrange("(c p) h -> p c h", p=128)
                      [:, :, hx * 128:(hx + 1) * 128], sem="lq1",
                      waits=[tok_scratch, st.get("w1q_free")])
            bank = 6 + (hx % 2)
            mms = []
            if split:
                for kd in range(CCH):
                    mms.append((w1q[:, kd, :],
                                xqr[slot][:, kd, base:base + CL],
                                slice(0, CL)))
                for kd in range(CCH):
                    mms.append((w1q[:, kd, :],
                                xqr[(j + 1) % 2][:, kd, 0:2],
                                slice(CL, CL + 2)))
            else:
                for kd in range(CCH):
                    mms.append((w1q[:, kd, :],
                                xqr[slot][:, kd, base:base + CL + 2],
                                slice(0, CL + 2)))
            th1 = mm_group(bank, mms,
                           waits=[twq, tok_x1_ready[j], seam_tok,
                                  bank_free[bank]])
            st["w1q_free"] = th1
            t_ev = vop(lambda v, hx=hx, bank=bank: v.tensor_tensor(
                htile_f[:, hx * (CL + 2):hx * (CL + 2) + 136],
                ps[bank][:, 0:136],
                d1b[:, 0:136], op=OP.mult),
                [th1, K(t_d1), tok_scratch if (jc == 0 and hx == 0) else None,
                 st.get("tileC_free")])
            bank_free[bank] = t_ev
            t_c3 = vop(lambda v, hx=hx: v.tensor_scalar(
                uc[:, 0:CL], htile[:, hx, 0:CL], cw3[:, hx, 0:1],
                None, op0=OP.mult), [t_ev, t_cw3])
            t_c3 = vop(lambda v, hx=hx: v.scalar_tensor_tensor(
                uc[:, 0:CL], htile[:, hx, 1:CL + 1], cw3[:, hx, 1:2],
                uc[:, 0:CL], op0=OP.mult, op1=OP.add), [t_c3])
            t_c3 = vop(lambda v, hx=hx: v.scalar_tensor_tensor(
                uc[:, 0:CL], htile[:, hx, 2:CL + 2], cw3[:, hx, 2:3],
                uc[:, 0:CL], op0=OP.mult, op1=OP.add), [t_c3])
            t_si = sop(lambda s, hx=hx: s.activation(
                htile[:, hx, 1:CL + 1], uc[:, 0:CL], ACTF.Silu,
                bias=cbs[:, hx:hx + 1], scale=1.0), [t_c3, t_cb])
            t_h_done = t_si
            t_max = vop(lambda v, hx=hx: v.tensor_tensor(
                M1[:, 0:CL], M1[:, 0:CL], htile[:, hx, 1:CL + 1],
                op=OP.max), [t_si, t_max])
            t_max = vop(lambda v, hx=hx: v.scalar_tensor_tensor(
                M1[:, 0:CL], htile[:, hx, 1:CL + 1], -1.0, M1[:, 0:CL],
                op0=OP.mult, op1=OP.max), [])
        ttr = transpose(ps[7][0:128, 0:128], M1[:, 0:CL],
                        waits=[t_max, bank_free[7]])
        t_red = vop(lambda v: v.reduce_max(mxt[:, 0:1], ps[7][:, 0:128],
                                           axis=AX),
                    [ttr, st.get("mx_free")])
        bank_free[7] = t_red
        st["m1_free"] = t_red

        def hscale(v):
            v.tensor_single_scalar(mxt[:, 0:1], mxt[:, 0:1], 1e-5, op=OP.max)
            v.drain()
            v.reciprocal(sht[:, 0:1], mxt[:, 0:1])
            return v.scalar_tensor_tensor(dht[:, 0:1], mxt[:, 0:1],
                                          1.0 / 127.0, mw2b[:, 0:1],
                                          op0=OP.mult, op1=OP.mult)
        t_hs = vop(hscale, [K(t_red), K(t_mw)])
        st["mx_free"] = t_hs
        t_rr = None
        for (srcap, dst) in ((sht[:, 0:1], shrow), (dht[:, 0:1], dhrow)):
            ttr = transpose(ps[7][0:1, 0:128], srcap,
                            waits=[t_hs, bank_free[7]])
            t_rr = sop(lambda s, dst=dst: s.activation(
                dst[0:1, 0:CL], ps[7][0:1, 0:128], ACTF.Copy),
                [ttr, st.get("shrow_free"), st.get("tileC_free")])
            bank_free[7] = t_rr
        t_bs = mm_group(7, [(ones_r[:], shrow[0:1, 0:CL], slice(0, CL))],
                        waits=[t_rr, bank_free[7]])
        t_shb = vop(lambda v: v.tensor_copy(shb[:, 0:CL], ps[7][:, 0:CL]),
                    [t_bs])
        bank_free[7] = t_shb
        t_bd = mm_group(7, [(ones_r[:], dhrow[0:1, 0:CL], slice(0, CL))],
                        waits=[t_rr, bank_free[7]])
        t_dhb = vop(lambda v: v.tensor_copy(dhb[:, 0:CL], ps[7][:, 0:CL]),
                    [t_bd])
        bank_free[7] = t_dhb
        st["shrow_free"] = t_dhb

        t_qh = None
        t_qprev = None
        for c in range(HCH):
            tu = vop(lambda v, c=c: v.tensor_tensor(
                uc[:, 0:CL], htile[:, c, 1:CL + 1], shb[:, 0:CL],
                op=OP.mult),
                [t_shb, t_h_done] if c == 0 else [t_qprev])
            tu = vop(lambda v: v.tensor_single_scalar(
                uc[:, 0:CL], uc[:, 0:CL], 127.0, op=OP.mult), [tu])
            tci = vop(lambda v: v.tensor_copy(cic[:, 0:CL], uc[:, 0:CL]),
                      [tu])
            tt2_ = vop(lambda v: v.tensor_tensor(
                uc[:, 0:CL], uc[:, 0:CL], cic[:, 0:CL], op=OP.subtract),
                [tci])
            tgp = vop(lambda v: v.tensor_single_scalar(
                gpc[:, 0:CL], uc[:, 0:CL], 0.5, op=OP.is_ge), [tt2_])
            tgn = vop(lambda v: v.tensor_single_scalar(
                uc[:, 0:CL], uc[:, 0:CL], -0.5, op=OP.is_le), [tgp])
            tco = vop(lambda v: v.tensor_tensor(
                gpc[:, 0:CL], gpc[:, 0:CL], uc[:, 0:CL], op=OP.subtract),
                [tgp, tgn])
            t_qh = vop(lambda v, c=c: v.tensor_tensor(
                qht[:, c, 0:CL], gpc[:, 0:CL], cic[:, 0:CL], op=OP.add),
                [tco])
            t_qprev = t_qh
        st["habs_free"] = t_qh

        t_z2 = None
        for mo in range(CCH):
            tw2 = dma("sync", w2q[:],
                      s_q2.ap().rearrange("(c p) d -> p c d", p=128)
                      [:, :, mo * 128:(mo + 1) * 128], sem="lq2",
                      waits=[tok_scratch, st.get("w2q_free")])
            bank = 6 + (mo % 2)
            mms = [(w2q[:, kc, :], qht[:, kc, 0:CL], slice(0, CL))
                   for kc in range(HCH)]
            th2 = mm_group(bank, mms,
                           waits=[tw2, t_qh, bank_free[bank]])
            st["w2q_free"] = th2
            t1 = vop(lambda v, mo=mo, bank=bank: v.tensor_tensor(
                z2fm[:, mo, 0:CL], ps[bank][:, 0:CL], dhb[:, 0:CL],
                op=OP.mult), [th2, t_dhb, st.get("z2_free")])
            if half == 0:
                t_z2 = vop(lambda v, mo=mo: v.tensor_tensor(
                    z2fm[:, mo, 0:CL], z2fm[:, mo, 0:CL],
                    x1r[slot][:, mo, 1:CL + 1], op=OP.add),
                    [t1, tok_x1_ready[j]])
            elif last:
                t_z2 = vop(lambda v, mo=mo: v.tensor_tensor(
                    z2fm[:, mo, 0:CL], z2fm[:, mo, 0:CL],
                    x1r[slot][:, mo, CL + 1:2 * CL + 1], op=OP.add),
                    [t1, tok_x1_ready[j]])
            else:
                t2a = vop(lambda v, mo=mo: v.tensor_tensor(
                    z2fm[:, mo, 0:CL - 1], z2fm[:, mo, 0:CL - 1],
                    x1r[slot][:, mo, CL + 1:2 * CL], op=OP.add),
                    [t1, tok_x1_ready[j]])
                t_z2 = vop(lambda v, mo=mo: v.tensor_tensor(
                    z2fm[:, mo, CL - 1:CL], z2fm[:, mo, CL - 1:CL],
                    x1r[(j + 1) % 2][:, mo, 0:1], op=OP.add),
                    [t2a, tok_xq_first[j + 1]])
            bank_free[bank] = t_z2
        if half == 1:
            tok_x1_free[j] = t_z2
        st["dhb_free"] = t_z2

        gate = st.get("tileC_free")
        t_ztm = None
        last_z2read = None
        for o in range(CCH):
            ttr = transpose(ps[7][0:128, 0:128], z2fm[:, o, 0:CL],
                            waits=[t_z2, bank_free[7]])
            last_z2read = ttr
            t_ztm = sop(lambda s, o=o: s.activation(
                ztm2[:, o * 128:(o + 1) * 128], ps[7][:, 0:128],
                ACTF.Copy), [ttr, gate])
            bank_free[7] = t_ztm
        t_s1 = vop(lambda v: v.reduce_sum(tinyD[:, 0:1], ztm2, axis=AX),
                   [t_ztm, gate])
        t_ss = sop(lambda s: s.activation(x2tv, ztm2, ACTF.Square,
                                          accum_out=tinyD[:, 1:2]),
                   [t_ztm, gate])

        def ln2_small(v):
            v.tensor_scalar(tinyD[:, 2:3], tinyD[:, 0:1], 1.0 / D, None,
                            op0=OP.mult)
            v.tensor_scalar(tinyD[:, 3:4], tinyD[:, 1:2], 1.0 / D, None,
                            op0=OP.mult)
            v.drain()
            v.tensor_tensor(tinyD[:, 4:5], tinyD[:, 2:3], tinyD[:, 2:3],
                            op=OP.mult)
            v.drain()
            v.tensor_tensor(tinyD[:, 5:6], tinyD[:, 3:4], tinyD[:, 4:5],
                            op=OP.subtract)
            v.drain()
            return v.tensor_single_scalar(tinyD[:, 5:6], tinyD[:, 5:6],
                                          1e-5, op=OP.add)
        t_var = vop(ln2_small, [K(t_s1), t_ss])
        t_sd = sop(lambda s: s.activation(tinyE[:, 0:1], tinyD[:, 5:6],
                                          ACTF.Sqrt), [t_var])

        def ln2_fin(v):
            v.reciprocal(tinyE[:, 1:2], tinyE[:, 0:1])
            v.drain()
            return v.scalar_tensor_tensor(
                tinyE[:, 2:3], tinyD[:, 2:3], -1.0, tinyE[:, 1:2],
                op0=OP.mult, op1=OP.mult)
        t_ab3 = vop(ln2_fin, [t_sd])
        t_out = vop(lambda v: v.tensor_scalar(
            ztm2, ztm2, tinyE[:, 1:2], tinyE[:, 2:3],
            op0=OP.mult, op1=OP.add), [K(t_ab3), t_ss])
        ts_ = dma("scalar", out_h.ap()[CL * jc:CL * (jc + 1), :],
                  ztm2, waits=[t_out])
        tok_store.append(ts_)
        st["tileC_free"] = ts_
        st["z2_free"] = last_z2read

    for jj in range(NB + 2):
        if jj < NB:
            emit_A(jj)
        if 0 <= jj - 1 < NB:
            emit_B(jj - 1)
        if 0 <= jj - 2 < NB:
            emit_C(2 * (jj - 2))
            emit_C(2 * (jj - 2) + 1)

    if debug_outputs:
        dbg_specs = [
            ("dbg_d1t", d1t[:], [128, NT_X1 + 1], F32),
            ("dbg_m1", mbuf[1][:], [128, CCH, MBW], F32),
            ("dbg_x1r1", x1r[1][:], [128, CCH, BL + 2], F32),
            ("dbg_xqr1", xqr[1][:], [128, CCH, BL + 2], BF16),
            ("dbg_scal", tinyC[:], [1, 8], F32),
            ("dbg_acc1", acc1[:], [128, 1], F32),
            ("dbg_acc2", acc2[:], [128, 1], F32),
            ("dbg_mw", mw1b[:], [128, 1], F32),
            ("dbg_sw", sw1b[:], [128, 1], F32),
            ("dbg_qht", qht[:], [128, HCH, CL], BF16),
            ("dbg_ht", htile[:], [128, HCH, CL + 2], F32),
        ]
        fin = [(s, c) for s, c in prog.cnt.items()
               if c > 0 and s != "ge" and s != "so"]
        for nm, ap, shp, dt in dbg_specs:
            hh = nc.dram_tensor(nm, shp, dt, kind="ExternalOutput")
            dma("scalar", hh.ap(), ap, waits=fin)
        for nm, scr, shp in [("dbg_s_w1h", s_w1h, [D, HID]),
                             ("dbg_s_q1", s_q1, [D, HID]),
                             ("dbg_s_owh", s_owh, [D, D])]:
            hh = nc.dram_tensor(nm, shp, BF16, kind="ExternalOutput")
            dma("scalar", hh.ap(), scr.ap(), waits=fin)

    prog.finalize()
    from contextlib import ExitStack
    es = ExitStack()
    sems = {k: es.enter_context(nc.semaphore(k))
            for k in list(DMA_SEMS) + ["pe", "ve", "se", "ge"]}
    blk = es.enter_context(nc.Block())

    def section(name):
        def f(eng):
            prog.run(name, eng, sems)
            if name == "scalar":
                eng.wait_ge(sems["so"], 16 * prog.cnt["so"])
                eng.wait_ge(sems["sw"], 16 * prog.cnt["sw"])
        return f

    blk.sync(section("sync"))
    blk.tensor(section("tensor"))
    blk.vector(section("vector"))
    blk.scalar(section("scalar"))
    blk.gpsimd(section("gpsimd"))
    es.close()
    return nc


_NC_CACHE = None


def kernel(**inputs):
    global _NC_CACHE
    x = np.asarray(inputs["L0_attn_x"])[0]
    xT = np.ascontiguousarray(x.T)
    w1t = np.ascontiguousarray(np.asarray(inputs["L0_ffn_w1"]).T)
    w2t = np.ascontiguousarray(np.asarray(inputs["L0_ffn_w2"]).T)
    cw3 = np.ascontiguousarray(np.asarray(inputs["L0_ffn_conv_w"])
                               .reshape(HID, 3))
    shared = {
        "in_w": np.ascontiguousarray(np.asarray(inputs["L0_attn_in_w"])),
        "mod_w1": np.ascontiguousarray(np.asarray(inputs["L0_attn_mod_w1"])),
        "mod_w2": np.ascontiguousarray(np.asarray(inputs["L0_attn_mod_w2"])),
        "out_w": np.ascontiguousarray(np.asarray(inputs["L0_attn_out_w"])),
        "hyk": np.ascontiguousarray(np.asarray(inputs["L0_attn_kernel"])),
        "w1t": w1t, "w2t": w2t, "cw3": cw3,
        "cb": np.ascontiguousarray(np.asarray(inputs["L0_ffn_conv_b"])),
        "b1": np.ascontiguousarray(np.asarray(inputs["L0_attn_mod_b1"])),
        "b2": np.ascontiguousarray(np.asarray(inputs["L0_attn_mod_b2"])),
        "inb": np.ascontiguousarray(np.asarray(inputs["L0_attn_in_b"])),
        "outb": np.ascontiguousarray(np.asarray(inputs["L0_attn_out_b"])),
        "idf": np.eye(128, dtype=np.float32),
        "ones_row": np.ones((1, 128), np.float32),
        "ones_col": np.ones((128, 1), np.float32),
    }
    in_maps = []
    for i in range(NCORE):
        a = i * OWN
        idx = np.arange(a - HALO, a + OWN + 1) % T
        xF = np.ascontiguousarray(xT[:, idx])
        if i == NCORE - 1:
            xF[:, -1] = 0.0
        m = dict(shared)
        m["xF"] = xF
        m["maskL"] = np.full((128, 1), 0.0 if i == 0 else 1.0, np.float32)
        m["maskR"] = np.full((128, 1), 0.0 if i == NCORE - 1 else 1.0,
                             np.float32)
        in_maps.append(m)

    if _NC_CACHE is None:
        _NC_CACHE = build_program()
    res = run_bass_kernel_spmd(_NC_CACHE, in_maps, core_ids=list(range(NCORE)))
    out = np.concatenate([np.asarray(r["out"]) for r in res.results], axis=0)
    return out[None, :, :]


# version 09 (final)
# speedup vs baseline: 1.2662x; 1.1391x over previous
import numpy as np
import concourse.bass as bass
import concourse.mybir as mybir
from concourse.bass_utils import run_bass_kernel_spmd

F32 = mybir.dt.float32
BF16 = mybir.dt.bfloat16
I32 = mybir.dt.int32
OP = mybir.AluOpType
ACTF = mybir.ActivationFunctionType
AX = mybir.AxisListType.X

T = 16384
D = 1024
HID = 4096
NCORE = 8
OWN = T // NCORE
HALO = 64
KT = 64
R = OWN + 2
M = OWN + HALO + 1
NB = 8
BL = OWN // NB
CL = 128
NCC = OWN // CL
HC = 256
NHC = HID // HC
CCH = 8
HCH = HID // 128
NT_X1 = 17

BLEN = [BL] * (NB - 1) + [BL + 2]
FLEN = [BL + HALO - 1] + [BL] * (NB - 2) + [BL + 2]
FOFF = [0] + [HALO - 1] * (NB - 1)
FSRC = [0] + [BL * j + HALO - 1 for j in range(1, NB)]
MBW = HALO - 1 + BL + 2

ENGINES = ("sync", "tensor", "vector", "scalar", "gpsimd")
DMA_SEMS = ("lc", "lx", "lw1", "lw2", "liw", "low", "lq1", "lq2", "lpr",
            "lg", "sw", "sw2", "so")


DRAIN_SEMS = ("ve", "se", "ge")
OWN_SEM = {"vector": "ve", "scalar": "se", "tensor": "pe", "gpsimd": "ge"}


def K(tok):
    return None if tok is None else ("KEEP", tok)


class Prog:
    def __init__(self):
        self.ops = {e: [] for e in ENGINES}
        self.cnt = {s: 0 for s in DMA_SEMS}
        self.cnt.update({"pe": 0, "ve": 0, "se": 0, "ge": 0})
        self.final = False

    def emit(self, eng, fn, waits=(), inc=None):
        if inc is None:
            tok = None
        else:
            self.cnt[inc] += 1
            tok = (inc, self.cnt[inc])
        own = OWN_SEM.get(eng)
        w2 = []
        for t in waits:
            if t is None:
                continue
            if t[0] == "KEEP":
                w2.append(t[1])
            elif t[0] != own:
                w2.append(t)
        self.ops[eng].append((fn, w2, tok))
        return tok

    def finalize(self):
        used = set()
        for e in ENGINES:
            for _, waits, _ in self.ops[e]:
                used.update(waits)
        kept = {s: [] for s in DRAIN_SEMS}
        for e in ENGINES:
            for _, _, tok in self.ops[e]:
                if tok is not None and tok[0] in DRAIN_SEMS and tok in used:
                    kept[tok[0]].append(tok[1])
        for s in kept:
            kept[s].sort()
        import bisect
        def remap(tok):
            s, v = tok
            if s not in DRAIN_SEMS:
                return tok
            return (s, bisect.bisect_right(kept[s], v))
        newops = {}
        for e in ENGINES:
            lst = []
            for fn, waits, tok in self.ops[e]:
                nw = [remap(t) for t in waits]
                if tok is None:
                    lst.append((fn, nw, None, False))
                elif tok[0] in DRAIN_SEMS:
                    if tok in used:
                        lst.append((fn, nw, remap(tok), True))
                    else:
                        lst.append((fn, nw, None, False))
                else:
                    lst.append((fn, nw, tok, False))
            newops[e] = lst
        self.ops = newops
        for s in DRAIN_SEMS:
            self.cnt[s] = len(kept[s])
        self.final = True

    def run(self, name, eng, sems):
        assert self.final
        seen = {}
        for fn, waits, tok, drained in self.ops[name]:
            for (s, v) in waits:
                if seen.get(s, 0) < v:
                    eng.wait_ge(sems[s], 16 * v if s in DMA_SEMS else v)
                    seen[s] = v
            inst = fn(eng)
            if tok is not None:
                if drained:
                    eng.drain().then_inc(sems[tok[0]], 1)
                else:
                    inst.then_inc(sems[tok[0]], 16 if tok[0] in DMA_SEMS else 1)


def build_program(debug_outputs=False):
    nc = bass.Bass("TRN2", target_bir_lowering=False,
                   detect_race_conditions=False)

    def din(name, shape, dt=F32):
        return nc.dram_tensor(name, shape, dt, kind="ExternalInput")

    xF_h = din("xF", [D, M])
    inw_h = din("in_w", [128, 8, D])
    mw1_h = din("mod_w1", [128, 8, HID])
    mw2_h = din("mod_w2", [128, 32, D])
    outw_h = din("out_w", [128, 8, D])
    hyk_h = din("hyk", [D, KT])
    w1t_h = din("w1t", [128, 8, HID])
    w2t_h = din("w2t", [128, 32, D])
    cw3_h = din("cw3", [HID, 3])
    cb_h = din("cb", [HID])
    b1_h = din("b1", [HID])
    b2_h = din("b2", [D])
    inb_h = din("inb", [D])
    outb_h = din("outb", [D])
    idf_h = din("idf", [128, 128])
    ones_h = din("ones_row", [1, 128])
    onesc_h = din("ones_col", [128, 1])
    maskL_h = din("maskL", [128, 1])
    maskR_h = din("maskR", [128, 1])
    out_h = nc.dram_tensor("out", [OWN, D], F32, kind="ExternalOutput")

    s_w1h = nc.dram_tensor("s_w1h", [128, NHC, CCH, HC], BF16, kind="Internal")
    s_w1l = nc.dram_tensor("s_w1l", [128, NHC, CCH, HC], BF16, kind="Internal")
    s_w2h = nc.dram_tensor("s_w2h", [128, NHC, 2, D], BF16, kind="Internal")
    s_w2l = nc.dram_tensor("s_w2l", [128, NHC, 2, D], BF16, kind="Internal")
    s_inwh = nc.dram_tensor("s_inwh", [128, 8, 8, 128], BF16, kind="Internal")
    s_inwl = nc.dram_tensor("s_inwl", [128, 8, 8, 128], BF16, kind="Internal")
    s_owh = nc.dram_tensor("s_owh", [128, 8, 8, 128], BF16, kind="Internal")
    s_owl = nc.dram_tensor("s_owl", [128, 8, 8, 128], BF16, kind="Internal")
    s_q1 = nc.dram_tensor("s_q1", [128, HCH, CCH, 128], BF16, kind="Internal")
    s_q2 = nc.dram_tensor("s_q2", [128, CCH, HCH, 128], BF16, kind="Internal")

    A = nc.alloc_sbuf_tensor
    hyk = A("hyk_s", [128, CCH, KT], F32)
    cw3 = A("cw3_s", [128, HCH, 3], F32)
    cbs = A("cb_s", [128, HCH], F32)
    b1s = A("b1_s", [128, HCH], F32)
    b2s = A("b2_s", [128, CCH], F32)
    inbs = A("inb_s", [128, CCH], F32)
    outbs = A("outb_s", [128, CCH], F32)
    idf = A("idf_s", [128, 128], F32)
    ones_r = A("ones_r", [1, 128], F32)
    ones_c = A("ones_c", [128, 1], F32)
    maskL = A("maskL_s", [128, 1], F32)
    maskR = A("maskR_s", [128, 1], F32)
    sw1b = A("sw1b", [128, 1], F32)
    sw2b = A("sw2b", [128, 1], F32)
    mw1b = A("mw1b", [128, 1], F32)
    mw2b = A("mw2b", [128, 1], F32)
    d1t = A("d1t", [128, NT_X1 + 1], F32)
    acc1 = A("acc1", [128, 1], F32)
    acc2 = A("acc2", [128, 1], F32)
    acc_tmp = A("acc_tmp", [128, 1], F32)
    tinyA = A("tinyA", [128, 8], F32)
    tinyB = A("tinyB", [128, 8], F32)
    tinyD = A("tinyD", [128, 8], F32)
    tinyE = A("tinyE", [128, 8], F32)
    tinyC = A("tinyC", [1, 8], F32)

    x1r = [A(f"x1r{s}", [128, CCH, BL + 2], F32) for s in range(2)]
    xqr = [A(f"xqr{s}", [128, CCH, BL + 2], BF16) for s in range(2)]

    HXW = (MBW + 1) // 2
    xbf = A("xbf", [128, CCH, HXW], F32)
    xh = [A(f"xh{s}", [128, CCH, MBW], BF16) for s in range(2)]
    xl = [A(f"xl{s}", [128, CCH, MBW], BF16) for s in range(2)]
    mbuf = [A(f"mbuf{s}", [128, CCH, MBW], F32) for s in range(2)]
    w1ch = A("w1ch", [128, CCH, HC], BF16)
    w1cl = A("w1cl", [128, CCH, HC], BF16)
    w2ch = A("w2ch", [128, 2, D], BF16)
    w2cl = A("w2cl", [128, 2, D], BF16)
    g32 = A("g32", [128, 2, MBW], F32)
    gh = [A(f"gh{s}", [128, 2, MBW], BF16) for s in range(2)]
    gl = [A(f"gl{s}", [128, 2, MBW], BF16) for s in range(2)]
    inwch = A("inwch", [128, CCH, 128], BF16)
    inwcl = A("inwcl", [128, CCH, 128], BF16)

    y32 = A("y32", [128, CCH, BL + 2], F32)
    yh = A("yh", [128, CCH, BL + 2], BF16)
    yl = A("yl", [128, CCH, BL + 2], BF16)
    owh = A("owh", [128, CCH, 128], BF16)
    owl = A("owl", [128, CCH, 128], BF16)
    zfm = A("zfm", [128, CCH, BL + 2], F32)
    ztm = A("ztm", [128, D], F32)
    x1t = A("x1t", [128, D], F32)
    uq = A("uq", [128, D], F32)
    ciq = A("ciq", [128, D], I32)
    gpq = A("gpq", [128, D], BF16)

    htile_f = A("htile", [128, HCH * (CL + 2) + 8], F32)
    htile = htile_f[:, 0:HCH * (CL + 2)].rearrange("p (a b) -> p a b", a=HCH)
    qht = A("qht", [128, HCH, CL], BF16)
    w1q = A("w1q", [128, CCH, 128], BF16)
    w2q = A("w2q", [128, HCH, 128], BF16)
    z2fm = A("z2fm", [128, CCH, CL], F32)
    shb = A("shb", [128, CL], F32)
    dhb = A("dhb", [128, CL], F32)
    d1b = A("d1b", [128, 136], F32)
    M1 = A("M1", [128, CL], F32)
    uc = A("uc", [128, CL], F32)
    cic = A("cic", [128, CL], I32)
    gpc = A("gpc", [128, CL], BF16)
    mxt = A("mxt", [128, 1], F32)
    sht = A("sht", [128, 1], F32)
    dht = A("dht", [128, 1], F32)
    d1row = A("d1row", [1, CL + 2], F32)
    shrow = A("shrow", [1, CL], F32)
    dhrow = A("dhrow", [1, CL], F32)

    htf = htile_f[:, 0:HCH * (CL + 2)]
    qhf = qht[:].rearrange("p a b -> p (a b)")
    wfp = htf[:, 0:1024].rearrange("p (a b) -> p a b", a=CCH)
    upv = htf[:, 1024:2048].rearrange("p (a b) -> p a b", a=CCH)
    gppv = htf[:, 2048:3072].rearrange("p (a b) -> p a b", a=CCH)
    gnpv = htf[:, 3072:4096].rearrange("p (a b) -> p a b", a=CCH)
    qwp = qhf[:, 0:2048].rearrange("p (a b) -> p a b", a=CCH)
    x2tv = htf[:, 0:1024]
    ztm2 = htf[:, 1024:2048]

    ps = [nc.alloc_psum_tensor(f"ps{i}", [128, 512], F32) for i in range(8)]

    prog = Prog()
    E = prog.emit

    def mm_group(bank, mms, waits=(), inc="pe"):
        def fn(te, bank=bank, mms=list(mms)):
            last = None
            n = len(mms)
            for i, (lh, rh, cs) in enumerate(mms):
                last = te.matmul(ps[bank][:, cs], lh, rh,
                                 start=(i == 0), stop=(i == n - 1))
            return last
        return E("tensor", fn, waits, inc)

    def transpose(bank_ap, src_ap, waits=(), inc="pe"):
        k = src_ap.shape[0]
        def fn(te, b=bank_ap, s=src_ap, k=k):
            return te.matmul(b, s, idf[0:k, 0:k], is_transpose=True,
                             start=True, stop=True)
        return E("tensor", fn, waits, inc)

    def dma(eng, out_ap, in_ap, waits=(), slow=False, sem=None):
        if sem is None:
            sem = "lc" if eng == "sync" else "so"
        def fn(e, o=out_ap, i=in_ap, slow=slow):
            return e.dma_start(out=o, in_=i, allow_slow_non_contiguous=slow)
        return E(eng, fn, waits, sem)

    def vop(fn, waits=(), inc="ve"):
        return E("vector", fn, waits, inc)

    def sop(fn, waits=(), inc="se"):
        return E("scalar", fn, waits, inc)

    def passes3(wh, wl, ah, al):
        return [(wh, ah), (wh, al), (wl, ah)]

    st = {"gfree": [None, None], "g_ready": [None, None]}

    dma("sync", idf[:], idf_h.ap())
    t_ones = dma("sync", ones_r[:], ones_h.ap())
    t_onesc = dma("sync", ones_c[:], onesc_h.ap())
    t_mskL = dma("sync", maskL[:], maskL_h.ap())
    t_mskR = dma("sync", maskR[:], maskR_h.ap())
    t_hyk = dma("sync", hyk[:], hyk_h.ap().rearrange("(c p) k -> p c k", p=128))
    t_cw3 = dma("sync", cw3[:], cw3_h.ap().rearrange("(c p) k -> p c k", p=128))
    t_cb = dma("sync", cbs[:], cb_h.ap().rearrange("(c p) -> p c", p=128),
               slow=True)
    t_b1 = dma("sync", b1s[:], b1_h.ap().rearrange("(c p) -> p c", p=128),
               slow=True)
    t_b2 = dma("sync", b2s[:], b2_h.ap().rearrange("(c p) -> p c", p=128),
               slow=True)
    t_inb = dma("sync", inbs[:], inb_h.ap().rearrange("(c p) -> p c", p=128),
                slow=True)
    t_outb = dma("sync", outbs[:], outb_h.ap().rearrange("(c p) -> p c", p=128),
                 slow=True)

    scr_w1_tok = [[] for _ in range(NHC)]
    scr_w2_tok = [[] for _ in range(NHC)]
    wf2 = [htf[:, 0:1024], htf[:, 1024:2048]]
    upf = htf[:, 2048:3072]
    gppf = htf[:, 3072:4096]
    gnpf = uq[:]
    qw2 = [qhf[:, 0:2048], qhf[:, 2048:4096]]
    SW = ["sw", "sw2"]

    def r4(ap, qa):
        return ap if qa is None else ap.rearrange("p (a b) -> p a b", a=qa)

    cons = [None, None]
    qfree = [None, None]
    pidx = [0]

    def hilo_chunk(src_ap, dsth, dstl, qa):
        b = pidx[0] % 2
        pidx[0] += 1
        tl = dma("sync", wf2[b], src_ap, sem="lpr", waits=[cons[b]])
        th = sop(lambda s, b=b: s.activation(qw2[b][:, 0:1024], wf2[b],
                                             ACTF.Copy), [tl, qfree[b]])
        tlo = vop(lambda v, b=b: v.tensor_tensor(qw2[b][:, 1024:2048], wf2[b],
                                                 qw2[b][:, 0:1024],
                                                 op=OP.subtract), [tl, th])
        cons[b] = tlo
        dma("scalar", dsth, r4(qw2[b][:, 0:1024], qa), waits=[th], sem=SW[b])
        ts = dma("scalar", dstl, r4(qw2[b][:, 1024:2048], qa),
                 waits=[tlo], sem=SW[b])
        qfree[b] = ts
        return ts

    for hs in range(4):
        grp = []
        for kd in range(CCH):
            grp.append(hilo_chunk(
                mw1_h.ap()[:, kd, hs * 1024:(hs + 1) * 1024],
                s_w1h.ap()[:, 4 * hs:4 * hs + 4, kd, :],
                s_w1l.ap()[:, 4 * hs:4 * hs + 4, kd, :], 4))
        for hc in range(4 * hs, 4 * hs + 4):
            scr_w1_tok[hc] = grp[-2:]
    for r in range(32):
        ts = hilo_chunk(mw2_h.ap()[:, r, :],
                        s_w2h.ap()[:, r // 2, r % 2, :],
                        s_w2l.ap()[:, r // 2, r % 2, :], None)
        if r % 2 == 1:
            scr_w2_tok[r // 2] = [ts_prev, ts]
        ts_prev = ts
    for kd in range(CCH):
        hilo_chunk(inw_h.ap()[:, kd, :],
                   s_inwh.ap()[:, :, kd, :], s_inwl.ap()[:, :, kd, :], 8)
    for kd in range(CCH):
        hilo_chunk(outw_h.ap()[:, kd, :],
                   s_owh.ap()[:, :, kd, :], s_owl.ap()[:, :, kd, :], 8)
    toks_hilo = [qfree[0], qfree[1]]

    vop(lambda v: v.memset(acc1[:], 0.0))
    vop(lambda v: v.memset(acc2[:], 0.0))
    mchunks = ([(w1t_h.ap()[:, kd, hs * 1024:(hs + 1) * 1024], acc1)
                for hs in range(4) for kd in range(CCH)] +
               [(w2t_h.ap()[:, r, :], acc2) for r in range(32)])
    t_red_prev = None
    tok_abs = []
    for i, (src_ap, accs) in enumerate(mchunks):
        b = pidx[0] % 2
        pidx[0] += 1
        tl = dma("gpsimd", wf2[b], src_ap, sem="lg", waits=[cons[b]])
        t_ab = sop(lambda s, b=b: s.activation(upf, wf2[b], ACTF.Abs),
                   [tl, t_red_prev])
        cons[b] = t_ab
        t_red_prev = vop(lambda v, a=accs: (
            v.reduce_sum(acc_tmp[:], upf, axis=AX),
            v.drain(),
            v.tensor_tensor(a[:], a[:], acc_tmp[:], op=OP.add))[-1],
            [t_ab])
        if i == 31 or i == 63:
            tok_abs.append(t_red_prev)

    def sum_fn(te):
        te.matmul(ps[6][0:1, 0:1], acc1[:], ones_c[:], start=True, stop=True)
        return te.matmul(ps[6][0:1, 1:2], acc2[:], ones_c[:], start=True,
                         stop=True)
    t_sum = E("tensor", sum_fn, [tok_abs[1], t_onesc], "pe")

    def mkscales(v):
        v.tensor_scalar(tinyC[0:1, 0:1], ps[6][0:1, 0:1], 1.0 / (D * HID),
                        1e-5, op0=OP.mult, op1=OP.max)
        v.tensor_scalar(tinyC[0:1, 1:2], ps[6][0:1, 1:2], 1.0 / (D * HID),
                        1e-5, op0=OP.mult, op1=OP.max)
        v.drain()
        return v.reciprocal(tinyC[0:1, 2:4], tinyC[0:1, 0:2])
    t_sc = vop(mkscales, [t_sum])
    t_bc = mm_group(6, [(ones_r[:], tinyC[0:1, 0:4], slice(0, 4))],
                    waits=[t_sc, t_ones])
    t_mw = vop(lambda v: (v.tensor_copy(mw1b[:], ps[6][:, 0:1]),
                          v.tensor_copy(mw2b[:], ps[6][:, 1:2]),
                          v.tensor_copy(sw1b[:], ps[6][:, 2:3]),
                          v.tensor_copy(sw2b[:], ps[6][:, 3:4]))[-1], [t_bc])
    bank_free = [None] * 8
    bank_free[6] = t_mw

    qchunks = ([(w1t_h.ap()[:, kd, hs * 1024:(hs + 1) * 1024], sw1b,
                 s_q1.ap()[:, 8 * hs:8 * hs + 8, kd, :], 8)
                for hs in range(4) for kd in range(CCH)] +
               [(w2t_h.ap()[:, r, :], sw2b, s_q2.ap()[:, :, r, :], 8)
                for r in range(32)])
    for src_ap, sc, dst, qa in qchunks:
        b = pidx[0] % 2
        pidx[0] += 1
        tl = dma("gpsimd", wf2[b], src_ap, sem="lg", waits=[cons[b]])
        tu = vop(lambda v, b=b, sc=sc: v.tensor_scalar(
            upf, wf2[b], sc[:, 0:1], None, op0=OP.mult), [tl, K(t_mw)])
        cons[b] = tu
        tg = vop(lambda v: v.tensor_single_scalar(
            gppf, upf, 0.5, op=OP.is_ge), [tu])
        tn = vop(lambda v: v.tensor_single_scalar(
            gnpf, upf, -0.5, op=OP.is_le), [tg])
        tq = vop(lambda v, b=b: v.tensor_tensor(
            qw2[b][:, 0:1024], gppf, gnpf, op=OP.subtract),
            [tn, qfree[b]])
        ts = dma("scalar", dst, r4(qw2[b][:, 0:1024], qa),
                 waits=[tq], sem=SW[b])
        qfree[b] = ts
    toks_scratch = [qfree[0], qfree[1]]

    tok_m_ready = [None] * NB
    tok_conv_done = [None] * NB
    tok_x_ready = [None] * NB
    tok_xfree = [None] * NB
    tok_x1_ready = [None] * NB
    tok_x1_free = [None] * NB
    tok_y32_free = [None] * NB
    tok_yhl_free = [None] * NB
    tok_xq_first = [None] * NB
    tok_d1_ready = [None] * NB
    tok_store = []

    def emit_A(j):
        slot = j % 2
        fs, fl, fo = FSRC[j], FLEN[j], FOFF[j]
        xv = xF_h.ap().rearrange("(c p) m -> p c m", p=128)
        t_lo = None
        off = 0
        while off < fl:
            ln = min(HXW, fl - off)
            t_xl = dma("sync", xbf[:, :, 0:ln],
                       xv[:, :, fs + off:fs + off + ln], sem="lx",
                       waits=[st.get("xbf_free"), t_hyk])
            t_hi = sop(lambda s, off=off, ln=ln: s.activation(
                xh[slot][:, :, fo + off:fo + off + ln], xbf[:, :, 0:ln],
                ACTF.Copy),
                [t_xl, tok_xfree[j - 2] if j >= 2 else None])
            t_lo = vop(lambda v, off=off, ln=ln: v.tensor_tensor(
                xl[slot][:, :, fo + off:fo + off + ln], xbf[:, :, 0:ln],
                xh[slot][:, :, fo + off:fo + off + ln], op=OP.subtract),
                [t_xl, t_hi, tok_xfree[j - 2] if j >= 2 else None])
            st["xbf_free"] = t_lo
            off += ln
        tok_x_ready[j] = t_lo

        t_seam = None
        if j > 0:
            t_seam = vop(lambda v: v.tensor_copy(
                mbuf[slot][:, :, 0:HALO - 1],
                mbuf[(j - 1) % 2][:, :, BL:BL + HALO - 1]),
                [tok_m_ready[j - 1],
                 tok_conv_done[j - 2] if j >= 2 else None])

        m_acc = [None] * CCH

        def emit_mod(hc, rot, g_ready, w2_tok):
            for mo in range(CCH):
                bank = 2 + (mo % 2)
                mms = []
                for (wop, gop) in passes3(w2ch, w2cl, gh[rot], gl[rot]):
                    for kc in range(2):
                        mms.append((wop[:, kc, mo * 128:(mo + 1) * 128],
                                    gop[:, kc, 0:fl], slice(0, fl)))
                tmod = mm_group(bank, mms,
                                waits=[w2_tok, g_ready, bank_free[bank]])
                if hc == 0:
                    tev = sop(lambda s, mo=mo, bank=bank: s.activation(
                        mbuf[slot][:, mo, fo:fo + fl], ps[bank][:, 0:fl],
                        ACTF.Copy),
                        [tmod, t_seam,
                         tok_conv_done[j - 2] if j >= 2 else None])
                elif hc == NHC - 1:
                    tev = vop(lambda v, mo=mo, bank=bank:
                              v.scalar_tensor_tensor(
                                  mbuf[slot][:, mo, fo:fo + fl],
                                  ps[bank][:, 0:fl], b2s[:, mo:mo + 1],
                                  mbuf[slot][:, mo, fo:fo + fl],
                                  op0=OP.add, op1=OP.add),
                              [tmod, t_b2, m_acc[mo]])
                else:
                    tev = vop(lambda v, mo=mo, bank=bank: v.tensor_tensor(
                        mbuf[slot][:, mo, fo:fo + fl], ps[bank][:, 0:fl],
                        mbuf[slot][:, mo, fo:fo + fl], op=OP.add),
                        [tmod, m_acc[mo]])
                m_acc[mo] = tev
                bank_free[bank] = tev
            st["gfree"][rot] = bank_free[2 + ((CCH - 1) % 2)]
            st["w2c_free"] = bank_free[2 + ((CCH - 1) % 2)]

        for hc in range(NHC + 1):
            if hc < NHC:
                dma("sync", w1ch[:], s_w1h.ap()[:, hc], sem="lw1",
                    waits=[st.get("w1c_free")] + scr_w1_tok[hc])
                w1_tok = dma("sync", w1cl[:], s_w1l.ap()[:, hc], sem="lw1")
                rot = hc % 2
                g_evs = []
                for mc in range(2):
                    bank = mc
                    mms = []
                    for (wop, xop) in passes3(w1ch, w1cl, xh[slot], xl[slot]):
                        for kd in range(CCH):
                            mms.append((wop[:, kd, mc * 128:(mc + 1) * 128],
                                        xop[:, kd, fo:fo + fl], slice(0, fl)))
                    tg = mm_group(bank, mms,
                                  waits=[w1_tok, tok_x_ready[j],
                                         bank_free[bank]])
                    te = sop(lambda s, mc=mc, hc=hc: s.activation(
                        g32[:, mc, 0:fl], ps[mc][:, 0:fl], ACTF.Gelu,
                        bias=b1s[:, 2 * hc + mc:2 * hc + mc + 1], scale=1.0),
                        [tg, t_b1, st.get("g32_free")])
                    bank_free[bank] = te
                    g_evs.append(te)
                st["w1c_free"] = g_evs[1]
                tgh = sop(lambda s, rot=rot: s.activation(
                    gh[rot][:, :, 0:fl], g32[:, :, 0:fl], ACTF.Copy),
                    [g_evs[0], g_evs[1], st["gfree"][rot]])
                tgl = vop(lambda v, rot=rot: v.tensor_tensor(
                    gl[rot][:, :, 0:fl], g32[:, :, 0:fl],
                    gh[rot][:, :, 0:fl], op=OP.subtract),
                    [tgh, st["gfree"][rot]])
                st["g32_free"] = tgl
            if hc >= 1:
                phc, prot, pg, pw2 = st["pend_mod"]
                emit_mod(phc, prot, pg, pw2)
            if hc < NHC:
                dma("sync", w2ch[:], s_w2h.ap()[:, hc], sem="lw2",
                    waits=[st.get("w2c_free")] + scr_w2_tok[hc])
                w2_tok = dma("sync", w2cl[:], s_w2l.ap()[:, hc], sem="lw2")
                st["pend_mod"] = (hc, rot, tgl, w2_tok)

        t_m = None
        for mo in range(CCH):
            bank = 2 + (mo % 2)
            dma("sync", inwch[:], s_inwh.ap()[:, mo], sem="liw",
                waits=toks_hilo + [st.get("inw_free")])
            tiw = dma("sync", inwcl[:], s_inwl.ap()[:, mo], sem="liw")
            mms = []
            for (wop, xop) in passes3(inwch, inwcl, xh[slot], xl[slot]):
                for kd in range(CCH):
                    mms.append((wop[:, kd, :],
                                xop[:, kd, fo:fo + fl], slice(0, fl)))
            txp = mm_group(bank, mms,
                           waits=[tiw, tok_x_ready[j], bank_free[bank]])
            st["inw_free"] = txp
            t_m = vop(lambda v, mo=mo, bank=bank: v.scalar_tensor_tensor(
                mbuf[slot][:, mo, fo:fo + fl], ps[bank][:, 0:fl],
                inbs[:, mo:mo + 1], mbuf[slot][:, mo, fo:fo + fl],
                op0=OP.add, op1=OP.mult), [txp, t_inb, m_acc[mo]])
            m_acc[mo] = t_m
            bank_free[bank] = t_m
        tok_m_ready[j] = m_acc[CCH - 1]

    def emit_B(j):
        slot = j % 2
        bl = BLEN[j]
        t_conv = None
        for c in range(CCH):
            t_conv = vop(lambda v, c=c: v.tensor_scalar(
                y32[:, c, 0:bl],
                mbuf[slot][:, c, HALO - 1:HALO - 1 + bl],
                hyk[:, c, 0:1], None, op0=OP.mult),
                [tok_m_ready[j],
                 tok_y32_free[j - 1] if j >= 1 else None])
        for k in range(1, KT):
            for c in range(CCH):
                t_conv = vop(lambda v, c=c, k=k: v.scalar_tensor_tensor(
                    y32[:, c, 0:bl],
                    mbuf[slot][:, c, HALO - 1 - k:HALO - 1 - k + bl],
                    hyk[:, c, k:k + 1], y32[:, c, 0:bl],
                    op0=OP.mult, op1=OP.add))
        tok_conv_done[j] = t_conv
        t_yh = sop(lambda s: s.activation(yh[:, :, 0:bl],
                                          y32[:, :, 0:bl], ACTF.Copy),
                   [t_conv, tok_yhl_free[j - 1] if j >= 1 else None])
        t_yl = vop(lambda v: v.tensor_tensor(yl[:, :, 0:bl],
                                             y32[:, :, 0:bl],
                                             yh[:, :, 0:bl],
                                             op=OP.subtract),
                   [t_yh, tok_yhl_free[j - 1] if j >= 1 else None])
        tok_y32_free[j] = t_yl
        tok_y_ready = t_yl

        t_z = None
        last_tz = None
        for mo in range(CCH):
            dma("sync", owh[:], s_owh.ap()[:, mo], sem="low",
                waits=toks_hilo + [st.get("ow_free")])
            towl = dma("sync", owl[:], s_owl.ap()[:, mo], sem="low")
            mms = []
            for (wop, yop) in passes3(owh, owl, yh, yl):
                for kc in range(CCH):
                    mms.append((wop[:, kc, :], yop[:, kc, 0:bl],
                                slice(0, bl)))
            tz = mm_group(4, mms, waits=[towl, tok_y_ready, bank_free[4]])
            st["ow_free"] = tz
            last_tz = tz
            t1 = vop(lambda v, mo=mo: v.scalar_tensor_tensor(
                zfm[:, mo, 0:bl], ps[4][:, 0:bl], outbs[:, mo:mo + 1],
                xh[slot][:, mo, HALO - 1:HALO - 1 + bl],
                op0=OP.add, op1=OP.add),
                [tz, t_outb, tok_x_ready[j], st.get("zfm_free")])
            t_z = vop(lambda v, mo=mo: v.tensor_tensor(
                zfm[:, mo, 0:bl], zfm[:, mo, 0:bl],
                xl[slot][:, mo, HALO - 1:HALO - 1 + bl], op=OP.add), [t1])
            bank_free[4] = t_z
        tok_yhl_free[j] = last_tz
        tok_xfree[j] = t_z

        ntile = 2 if j < NB - 1 else 3
        t_ring = None
        last_zread = None
        for tt in range(ntile):
            tc0 = 128 * tt
            tw = 128 if tt < 2 else 2
            gidx = 2 * j + tt
            gate = st.get("tileB_free")
            t_ztm = None
            for o in range(CCH):
                ttr = transpose(ps[5][0:tw, 0:128], zfm[:, o, tc0:tc0 + tw],
                                waits=[t_z, bank_free[5]])
                last_zread = ttr
                t_ztm = sop(lambda s, o=o, tw=tw: s.activation(
                    ztm[0:tw, o * 128:(o + 1) * 128], ps[5][0:tw, 0:128],
                    ACTF.Copy), [ttr, gate])
                bank_free[5] = t_ztm
            t_s1 = vop(lambda v, tw=tw: v.reduce_sum(
                tinyA[0:tw, 0:1], ztm[0:tw, :], axis=AX), [t_ztm, gate])
            t_ss = sop(lambda s, tw=tw: s.activation(
                x1t[0:tw, :], ztm[0:tw, :], ACTF.Square,
                accum_out=tinyA[0:tw, 1:2]), [t_ztm, gate])

            def ln_small(v, tw=tw):
                v.tensor_scalar(tinyA[0:tw, 2:3], tinyA[0:tw, 0:1], 1.0 / D,
                                None, op0=OP.mult)
                v.tensor_scalar(tinyA[0:tw, 3:4], tinyA[0:tw, 1:2], 1.0 / D,
                                None, op0=OP.mult)
                v.drain()
                v.tensor_tensor(tinyA[0:tw, 4:5], tinyA[0:tw, 2:3],
                                tinyA[0:tw, 2:3], op=OP.mult)
                v.drain()
                v.tensor_tensor(tinyA[0:tw, 5:6], tinyA[0:tw, 3:4],
                                tinyA[0:tw, 4:5], op=OP.subtract)
                v.drain()
                return v.tensor_single_scalar(tinyA[0:tw, 5:6],
                                              tinyA[0:tw, 5:6], 1e-5,
                                              op=OP.add)
            t_var = vop(ln_small, [K(t_s1), t_ss])
            t_sd = sop(lambda s, tw=tw: s.activation(
                tinyB[0:tw, 0:1], tinyA[0:tw, 5:6], ACTF.Sqrt), [t_var])

            def ln_fin(v, tw=tw):
                v.reciprocal(tinyB[0:tw, 1:2], tinyB[0:tw, 0:1])
                v.drain()
                return v.scalar_tensor_tensor(
                    tinyB[0:tw, 2:3], tinyA[0:tw, 2:3], -1.0,
                    tinyB[0:tw, 1:2], op0=OP.mult, op1=OP.mult)
            t_ab = vop(ln_fin, [t_sd])
            t_x1 = vop(lambda v, tw=tw: v.tensor_scalar(
                x1t[0:tw, :], ztm[0:tw, :], tinyB[0:tw, 1:2],
                tinyB[0:tw, 2:3], op0=OP.mult, op1=OP.add), [K(t_ab), t_ss])
            t_mx = vop(lambda v, tw=tw: v.reduce_max(
                tinyB[0:tw, 3:4], x1t[0:tw, :], axis=AX,
                apply_absolute_value=True), [t_x1])

            def qscale(v, tw=tw, gidx=gidx):
                v.tensor_single_scalar(tinyB[0:tw, 4:5], tinyB[0:tw, 3:4],
                                       1e-5, op=OP.max)
                v.drain()
                v.reciprocal(tinyB[0:tw, 5:6], tinyB[0:tw, 4:5])
                return v.scalar_tensor_tensor(
                    d1t[0:tw, gidx:gidx + 1], tinyB[0:tw, 4:5], 1.0 / 127.0,
                    mw1b[0:tw, 0:1], op0=OP.mult, op1=OP.mult)
            t_qs = vop(qscale, [K(t_mx), K(t_mw)])
            tok_d1_ready[j] = t_qs
            t_u = vop(lambda v, tw=tw: v.tensor_scalar(
                uq[0:tw, :], x1t[0:tw, :], tinyB[0:tw, 5:6], 127.0,
                op0=OP.mult, op1=OP.mult), [K(t_qs)])
            t_ci = vop(lambda v, tw=tw: v.tensor_copy(
                ciq[0:tw, :], uq[0:tw, :]), [t_u])
            t_t2 = vop(lambda v, tw=tw: v.tensor_tensor(
                uq[0:tw, :], uq[0:tw, :], ciq[0:tw, :], op=OP.subtract),
                [t_ci])
            t_gp = vop(lambda v, tw=tw: v.tensor_single_scalar(
                gpq[0:tw, :], uq[0:tw, :], 0.5, op=OP.is_ge), [t_t2])
            t_gn = vop(lambda v, tw=tw: v.tensor_single_scalar(
                uq[0:tw, :], uq[0:tw, :], -0.5, op=OP.is_le), [t_gp])
            t_co = vop(lambda v, tw=tw: v.tensor_tensor(
                gpq[0:tw, :], gpq[0:tw, :], uq[0:tw, :], op=OP.subtract),
                [t_gp, t_gn])
            t_q = vop(lambda v, tw=tw: v.tensor_tensor(
                uq[0:tw, :], gpq[0:tw, :], ciq[0:tw, :], op=OP.add), [t_co])
            ring_w = tok_x1_free[j - 2] if j >= 2 else None
            for o in range(CCH):
                ttr = transpose(ps[5][0:128, 0:tw],
                                uq[0:tw, o * 128:(o + 1) * 128],
                                waits=[t_q, bank_free[5]])
                t_e1 = sop(lambda s, o=o, tc0=tc0, tw=tw: s.activation(
                    xqr[slot][:, o, tc0:tc0 + tw], ps[5][:, 0:tw], ACTF.Copy),
                    [ttr, ring_w])
                bank_free[5] = t_e1
                ttr2 = transpose(ps[5][0:128, 0:tw],
                                 x1t[0:tw, o * 128:(o + 1) * 128],
                                 waits=[t_x1, t_q, bank_free[5]])
                t_e2 = sop(lambda s, o=o, tc0=tc0, tw=tw: s.activation(
                    x1r[slot][:, o, tc0:tc0 + tw], ps[5][:, 0:tw], ACTF.Copy),
                    [ttr2, ring_w])
                bank_free[5] = t_e2
                t_ring = t_e2
            st["tileB_free"] = t_ring
            if tt == 0:
                tok_xq_first[j] = t_ring
        tok_x1_ready[j] = t_ring
        st["zfm_free"] = last_zread

    def emit_C(jc):
        j = jc // 2
        slot = j % 2
        half = jc % 2
        base = CL * half
        last = (jc == NCC - 1)
        split = (half == 1 and not last)
        seam_tok = tok_xq_first[j + 1] if split else tok_x1_ready[j]
        t_r = None
        for (srcap, n0, n1, wk) in (
                (d1t[:, jc:jc + 1], 0, CL, tok_d1_ready[j]),
                (d1t[0:2, jc + 1:jc + 2], CL, CL + 2,
                 tok_d1_ready[min(j + (1 if jc % 2 == 1 else 0), NB - 1)])):
            npart = n1 - n0
            ttr = transpose(ps[6][0:1, 0:npart], srcap,
                            waits=[wk, bank_free[6]])
            t_r = sop(lambda s, n0=n0, n1=n1, npart=npart: s.activation(
                d1row[0:1, n0:n1], ps[6][0:1, 0:npart], ACTF.Copy),
                [ttr, st.get("d1row_free"), st.get("tileC_free")])
            bank_free[6] = t_r
        t_bcd = mm_group(6, [(ones_r[:], d1row[0:1, 0:CL + 2],
                              slice(0, CL + 2))],
                         waits=[t_r, bank_free[6]])
        st["d1row_free"] = t_bcd
        t_d1 = vop(lambda v: v.tensor_copy(d1b[:, 0:CL + 2],
                                           ps[6][:, 0:CL + 2]), [t_bcd])
        bank_free[6] = t_d1
        if jc == 0:
            t_d1 = vop(lambda v: v.tensor_scalar(
                d1b[:, 0:1], d1b[:, 0:1], maskL[:, 0:1], None,
                op0=OP.mult), [t_mskL])
        if last:
            t_d1 = vop(lambda v: v.tensor_scalar(
                d1b[:, CL + 1:CL + 2], d1b[:, CL + 1:CL + 2],
                maskR[:, 0:1], None, op0=OP.mult), [t_mskR])

        t_m1z = vop(lambda v: v.memset(M1[:], 0.0), [st.get("m1_free")])
        t_max = t_m1z
        t_h_done = None
        for hx in range(HCH):
            twq = dma("sync", w1q[:], s_q1.ap()[:, hx], sem="lq1",
                      waits=toks_scratch + [st.get("w1q_free")])
            bank = 6 + (hx % 2)
            mms = []
            if split:
                for kd in range(CCH):
                    mms.append((w1q[:, kd, :],
                                xqr[slot][:, kd, base:base + CL],
                                slice(0, CL)))
                for kd in range(CCH):
                    mms.append((w1q[:, kd, :],
                                xqr[(j + 1) % 2][:, kd, 0:2],
                                slice(CL, CL + 2)))
            else:
                for kd in range(CCH):
                    mms.append((w1q[:, kd, :],
                                xqr[slot][:, kd, base:base + CL + 2],
                                slice(0, CL + 2)))
            th1 = mm_group(bank, mms,
                           waits=[twq, tok_x1_ready[j], seam_tok,
                                  bank_free[bank]])
            st["w1q_free"] = th1
            t_ev = vop(lambda v, hx=hx, bank=bank: v.tensor_tensor(
                htile_f[:, hx * (CL + 2):hx * (CL + 2) + 136],
                ps[bank][:, 0:136],
                d1b[:, 0:136], op=OP.mult),
                [th1, K(t_d1)]
                + (toks_scratch if (jc == 0 and hx == 0) else [])
                + [st.get("tileC_free")])
            bank_free[bank] = t_ev
            t_c3 = vop(lambda v, hx=hx: v.tensor_scalar(
                uc[:, 0:CL], htile[:, hx, 0:CL], cw3[:, hx, 0:1],
                None, op0=OP.mult), [t_ev, t_cw3])
            t_c3 = vop(lambda v, hx=hx: v.scalar_tensor_tensor(
                uc[:, 0:CL], htile[:, hx, 1:CL + 1], cw3[:, hx, 1:2],
                uc[:, 0:CL], op0=OP.mult, op1=OP.add), [t_c3])
            t_c3 = vop(lambda v, hx=hx: v.scalar_tensor_tensor(
                uc[:, 0:CL], htile[:, hx, 2:CL + 2], cw3[:, hx, 2:3],
                uc[:, 0:CL], op0=OP.mult, op1=OP.add), [t_c3])
            t_si = sop(lambda s, hx=hx: s.activation(
                htile[:, hx, 1:CL + 1], uc[:, 0:CL], ACTF.Silu,
                bias=cbs[:, hx:hx + 1], scale=1.0), [t_c3, t_cb])
            t_h_done = t_si
            t_max = vop(lambda v, hx=hx: v.tensor_tensor(
                M1[:, 0:CL], M1[:, 0:CL], htile[:, hx, 1:CL + 1],
                op=OP.max), [t_si, t_max])
            t_max = vop(lambda v, hx=hx: v.scalar_tensor_tensor(
                M1[:, 0:CL], htile[:, hx, 1:CL + 1], -1.0, M1[:, 0:CL],
                op0=OP.mult, op1=OP.max), [])
        ttr = transpose(ps[7][0:128, 0:128], M1[:, 0:CL],
                        waits=[t_max, bank_free[7]])
        t_red = vop(lambda v: v.reduce_max(mxt[:, 0:1], ps[7][:, 0:128],
                                           axis=AX),
                    [ttr, st.get("mx_free")])
        bank_free[7] = t_red
        st["m1_free"] = t_red

        def hscale(v):
            v.tensor_single_scalar(mxt[:, 0:1], mxt[:, 0:1], 1e-5, op=OP.max)
            v.drain()
            v.reciprocal(sht[:, 0:1], mxt[:, 0:1])
            return v.scalar_tensor_tensor(dht[:, 0:1], mxt[:, 0:1],
                                          1.0 / 127.0, mw2b[:, 0:1],
                                          op0=OP.mult, op1=OP.mult)
        t_hs = vop(hscale, [K(t_red), K(t_mw)])
        st["mx_free"] = t_hs
        t_rr = None
        for (srcap, dst) in ((sht[:, 0:1], shrow), (dht[:, 0:1], dhrow)):
            ttr = transpose(ps[7][0:1, 0:128], srcap,
                            waits=[t_hs, bank_free[7]])
            t_rr = sop(lambda s, dst=dst: s.activation(
                dst[0:1, 0:CL], ps[7][0:1, 0:128], ACTF.Copy),
                [ttr, st.get("shrow_free"), st.get("tileC_free")])
            bank_free[7] = t_rr
        t_bs = mm_group(7, [(ones_r[:], shrow[0:1, 0:CL], slice(0, CL))],
                        waits=[t_rr, bank_free[7]])
        t_shb = vop(lambda v: v.tensor_copy(shb[:, 0:CL], ps[7][:, 0:CL]),
                    [t_bs])
        bank_free[7] = t_shb
        t_bd = mm_group(7, [(ones_r[:], dhrow[0:1, 0:CL], slice(0, CL))],
                        waits=[t_rr, bank_free[7]])
        t_dhb = vop(lambda v: v.tensor_copy(dhb[:, 0:CL], ps[7][:, 0:CL]),
                    [t_bd])
        bank_free[7] = t_dhb
        st["shrow_free"] = t_dhb

        t_qh = None
        t_qprev = None
        for c in range(HCH):
            tu = vop(lambda v, c=c: v.tensor_tensor(
                uc[:, 0:CL], htile[:, c, 1:CL + 1], shb[:, 0:CL],
                op=OP.mult),
                [t_shb, t_h_done] if c == 0 else [t_qprev])
            tu = vop(lambda v: v.tensor_single_scalar(
                uc[:, 0:CL], uc[:, 0:CL], 127.0, op=OP.mult), [tu])
            tci = vop(lambda v: v.tensor_copy(cic[:, 0:CL], uc[:, 0:CL]),
                      [tu])
            tt2_ = vop(lambda v: v.tensor_tensor(
                uc[:, 0:CL], uc[:, 0:CL], cic[:, 0:CL], op=OP.subtract),
                [tci])
            tgp = vop(lambda v: v.tensor_single_scalar(
                gpc[:, 0:CL], uc[:, 0:CL], 0.5, op=OP.is_ge), [tt2_])
            tgn = vop(lambda v: v.tensor_single_scalar(
                uc[:, 0:CL], uc[:, 0:CL], -0.5, op=OP.is_le), [tgp])
            tco = vop(lambda v: v.tensor_tensor(
                gpc[:, 0:CL], gpc[:, 0:CL], uc[:, 0:CL], op=OP.subtract),
                [tgp, tgn])
            t_qh = vop(lambda v, c=c: v.tensor_tensor(
                qht[:, c, 0:CL], gpc[:, 0:CL], cic[:, 0:CL], op=OP.add),
                [tco])
            t_qprev = t_qh
        st["habs_free"] = t_qh

        t_z2 = None
        for mo in range(CCH):
            tw2 = dma("sync", w2q[:], s_q2.ap()[:, mo], sem="lq2",
                      waits=toks_scratch + [st.get("w2q_free")])
            bank = 6 + (mo % 2)
            mms = [(w2q[:, kc, :], qht[:, kc, 0:CL], slice(0, CL))
                   for kc in range(HCH)]
            th2 = mm_group(bank, mms,
                           waits=[tw2, t_qh, bank_free[bank]])
            st["w2q_free"] = th2
            t1 = vop(lambda v, mo=mo, bank=bank: v.tensor_tensor(
                z2fm[:, mo, 0:CL], ps[bank][:, 0:CL], dhb[:, 0:CL],
                op=OP.mult), [th2, t_dhb, st.get("z2_free")])
            if half == 0:
                t_z2 = vop(lambda v, mo=mo: v.tensor_tensor(
                    z2fm[:, mo, 0:CL], z2fm[:, mo, 0:CL],
                    x1r[slot][:, mo, 1:CL + 1], op=OP.add),
                    [t1, tok_x1_ready[j]])
            elif last:
                t_z2 = vop(lambda v, mo=mo: v.tensor_tensor(
                    z2fm[:, mo, 0:CL], z2fm[:, mo, 0:CL],
                    x1r[slot][:, mo, CL + 1:2 * CL + 1], op=OP.add),
                    [t1, tok_x1_ready[j]])
            else:
                t2a = vop(lambda v, mo=mo: v.tensor_tensor(
                    z2fm[:, mo, 0:CL - 1], z2fm[:, mo, 0:CL - 1],
                    x1r[slot][:, mo, CL + 1:2 * CL], op=OP.add),
                    [t1, tok_x1_ready[j]])
                t_z2 = vop(lambda v, mo=mo: v.tensor_tensor(
                    z2fm[:, mo, CL - 1:CL], z2fm[:, mo, CL - 1:CL],
                    x1r[(j + 1) % 2][:, mo, 0:1], op=OP.add),
                    [t2a, tok_xq_first[j + 1]])
            bank_free[bank] = t_z2
        if half == 1:
            tok_x1_free[j] = t_z2
        st["dhb_free"] = t_z2

        gate = st.get("tileC_free")
        t_ztm = None
        last_z2read = None
        for o in range(CCH):
            ttr = transpose(ps[7][0:128, 0:128], z2fm[:, o, 0:CL],
                            waits=[t_z2, bank_free[7]])
            last_z2read = ttr
            t_ztm = sop(lambda s, o=o: s.activation(
                ztm2[:, o * 128:(o + 1) * 128], ps[7][:, 0:128],
                ACTF.Copy), [ttr, gate])
            bank_free[7] = t_ztm
        t_s1 = vop(lambda v: v.reduce_sum(tinyD[:, 0:1], ztm2, axis=AX),
                   [t_ztm, gate])
        t_ss = sop(lambda s: s.activation(x2tv, ztm2, ACTF.Square,
                                          accum_out=tinyD[:, 1:2]),
                   [t_ztm, gate])

        def ln2_small(v):
            v.tensor_scalar(tinyD[:, 2:3], tinyD[:, 0:1], 1.0 / D, None,
                            op0=OP.mult)
            v.tensor_scalar(tinyD[:, 3:4], tinyD[:, 1:2], 1.0 / D, None,
                            op0=OP.mult)
            v.drain()
            v.tensor_tensor(tinyD[:, 4:5], tinyD[:, 2:3], tinyD[:, 2:3],
                            op=OP.mult)
            v.drain()
            v.tensor_tensor(tinyD[:, 5:6], tinyD[:, 3:4], tinyD[:, 4:5],
                            op=OP.subtract)
            v.drain()
            return v.tensor_single_scalar(tinyD[:, 5:6], tinyD[:, 5:6],
                                          1e-5, op=OP.add)
        t_var = vop(ln2_small, [K(t_s1), t_ss])
        t_sd = sop(lambda s: s.activation(tinyE[:, 0:1], tinyD[:, 5:6],
                                          ACTF.Sqrt), [t_var])

        def ln2_fin(v):
            v.reciprocal(tinyE[:, 1:2], tinyE[:, 0:1])
            v.drain()
            return v.scalar_tensor_tensor(
                tinyE[:, 2:3], tinyD[:, 2:3], -1.0, tinyE[:, 1:2],
                op0=OP.mult, op1=OP.mult)
        t_ab3 = vop(ln2_fin, [t_sd])
        t_out = vop(lambda v: v.tensor_scalar(
            ztm2, ztm2, tinyE[:, 1:2], tinyE[:, 2:3],
            op0=OP.mult, op1=OP.add), [K(t_ab3), t_ss])
        ts_ = dma("scalar", out_h.ap()[CL * jc:CL * (jc + 1), :],
                  ztm2, waits=[t_out])
        tok_store.append(ts_)
        st["tileC_free"] = ts_
        st["z2_free"] = last_z2read

    for jj in range(NB + 2):
        if jj < NB:
            emit_A(jj)
        if 0 <= jj - 1 < NB:
            emit_B(jj - 1)
        if 0 <= jj - 2 < NB:
            emit_C(2 * (jj - 2))
            emit_C(2 * (jj - 2) + 1)

    if debug_outputs:
        dbg_specs = [
            ("dbg_d1t", d1t[:], [128, NT_X1 + 1], F32),
            ("dbg_m1", mbuf[1][:], [128, CCH, MBW], F32),
            ("dbg_x1r1", x1r[1][:], [128, CCH, BL + 2], F32),
            ("dbg_xqr1", xqr[1][:], [128, CCH, BL + 2], BF16),
            ("dbg_scal", tinyC[:], [1, 8], F32),
            ("dbg_acc1", acc1[:], [128, 1], F32),
            ("dbg_acc2", acc2[:], [128, 1], F32),
            ("dbg_mw", mw1b[:], [128, 1], F32),
            ("dbg_sw", sw1b[:], [128, 1], F32),
            ("dbg_qht", qht[:], [128, HCH, CL], BF16),
            ("dbg_ht", htile[:], [128, HCH, CL + 2], F32),
        ]
        fin = [(s, c) for s, c in prog.cnt.items()
               if c > 0 and s != "ge" and s != "so"]
        for nm, ap, shp, dt in dbg_specs:
            hh = nc.dram_tensor(nm, shp, dt, kind="ExternalOutput")
            dma("scalar", hh.ap(), ap, waits=fin)
        for nm, scr, shp in [("dbg_s_w1h", s_w1h, [128, NHC, CCH, HC]),
                             ("dbg_s_q1", s_q1, [128, HCH, CCH, 128]),
                             ("dbg_s_owh", s_owh, [128, 8, 8, 128])]:
            hh = nc.dram_tensor(nm, shp, BF16, kind="ExternalOutput")
            dma("scalar", hh.ap(), scr.ap(), waits=fin)

    prog.finalize()
    from contextlib import ExitStack
    es = ExitStack()
    sems = {k: es.enter_context(nc.semaphore(k))
            for k in list(DMA_SEMS) + ["pe", "ve", "se", "ge"]}
    blk = es.enter_context(nc.Block())

    def section(name):
        def f(eng):
            prog.run(name, eng, sems)
            if name == "scalar":
                eng.wait_ge(sems["so"], 16 * prog.cnt["so"])
                eng.wait_ge(sems["sw"], 16 * prog.cnt["sw"])
                eng.wait_ge(sems["sw2"], 16 * prog.cnt["sw2"])
        return f

    blk.sync(section("sync"))
    blk.tensor(section("tensor"))
    blk.vector(section("vector"))
    blk.scalar(section("scalar"))
    blk.gpsimd(section("gpsimd"))
    es.close()
    return nc


_NC_CACHE = None


def kernel(**inputs):
    global _NC_CACHE
    x = np.asarray(inputs["L0_attn_x"])[0]
    xT = np.ascontiguousarray(x.T)

    def pm(w):
        R, C = w.shape
        return np.ascontiguousarray(
            w.reshape(R // 128, 128, C).transpose(1, 0, 2))

    w1t = pm(np.asarray(inputs["L0_ffn_w1"]).T)
    w2t = pm(np.asarray(inputs["L0_ffn_w2"]).T)
    cw3 = np.ascontiguousarray(np.asarray(inputs["L0_ffn_conv_w"])
                               .reshape(HID, 3))
    shared = {
        "in_w": pm(np.asarray(inputs["L0_attn_in_w"])),
        "mod_w1": pm(np.asarray(inputs["L0_attn_mod_w1"])),
        "mod_w2": pm(np.asarray(inputs["L0_attn_mod_w2"])),
        "out_w": pm(np.asarray(inputs["L0_attn_out_w"])),
        "hyk": np.ascontiguousarray(np.asarray(inputs["L0_attn_kernel"])),
        "w1t": w1t, "w2t": w2t, "cw3": cw3,
        "cb": np.ascontiguousarray(np.asarray(inputs["L0_ffn_conv_b"])),
        "b1": np.ascontiguousarray(np.asarray(inputs["L0_attn_mod_b1"])),
        "b2": np.ascontiguousarray(np.asarray(inputs["L0_attn_mod_b2"])),
        "inb": np.ascontiguousarray(np.asarray(inputs["L0_attn_in_b"])),
        "outb": np.ascontiguousarray(np.asarray(inputs["L0_attn_out_b"])),
        "idf": np.eye(128, dtype=np.float32),
        "ones_row": np.ones((1, 128), np.float32),
        "ones_col": np.ones((128, 1), np.float32),
    }
    in_maps = []
    for i in range(NCORE):
        a = i * OWN
        idx = np.arange(a - HALO, a + OWN + 1) % T
        xF = np.ascontiguousarray(xT[:, idx])
        if i == NCORE - 1:
            xF[:, -1] = 0.0
        m = dict(shared)
        m["xF"] = xF
        m["maskL"] = np.full((128, 1), 0.0 if i == 0 else 1.0, np.float32)
        m["maskR"] = np.full((128, 1), 0.0 if i == NCORE - 1 else 1.0,
                             np.float32)
        in_maps.append(m)

    if _NC_CACHE is None:
        _NC_CACHE = build_program()
    res = run_bass_kernel_spmd(_NC_CACHE, in_maps, core_ids=list(range(NCORE)))
    out = np.concatenate([np.asarray(r["out"]) for r in res.results], axis=0)
    return out[None, :, :]
